# version 24
# baseline (speedup 1.0000x reference)
import math
from contextlib import ExitStack

import numpy as np
import concourse.bass as bass
import concourse.mybir as mybir
from concourse.bass_utils import run_bass_kernel_spmd

F32 = mybir.dt.float32
BF16 = mybir.dt.bfloat16
AF = mybir.ActivationFunctionType
ALU = mybir.AluOpType
AX = mybir.AxisListType

S = 2048
DM = 2048
HD = 128
NH = 16
NKV = 4
WIN = 512
NB = 2
NCORES = 8
KT = DM // 128
NT = S // 128
EPS_LN = 1e-5
EPS_RMS = 1.1920928955078125e-07
THETA = 10000.0
NE = 8
RANK = 16
SCALING = 32.0 / RANK
ISD = 1.0 / math.sqrt(HD)

ENGS = ("tensor", "vector", "scalar", "gpsimd", "sync")


class _Op:
    __slots__ = ("eng", "emit", "deps", "val", "is_dma", "sem", "tag")

    def __init__(self, eng, emit, is_dma, tag):
        self.eng = eng
        self.emit = emit
        self.deps = []
        self.val = None
        self.sem = None
        self.is_dma = is_dma
        self.tag = tag


class Res:
    __slots__ = ("name", "last_w", "readers")

    def __init__(self, name):
        self.name = name
        self.last_w = None
        self.readers = []


class Prog:
    def __init__(self):
        self.seq = {e: [] for e in ENGS}
        self.eng_ctr = {e: 0 for e in ENGS}
        self.dma_sems = {}
        self.dma_eng = {}
        self.last_store = {}

    def channel(self, name, eng):
        self.dma_eng[name] = eng

    def op(self, eng, emit, reads=(), writes=(), tag="", deps=()):
        o = _Op(eng, emit, False, tag)
        self.eng_ctr[eng] += 1
        o.val = self.eng_ctr[eng]
        o.sem = "E_" + eng
        self._wire(o, reads, writes, deps)
        self.seq[eng].append(o)
        return o

    def dma(self, ch, slot, emit, reads=(), writes=(), tag="", deps=()):
        key = (ch, slot)
        if key not in self.dma_sems:
            self.dma_sems[key] = ["D_%s_%s" % (ch, slot), 0, None]
        ent = self.dma_sems[key]
        ent[1] += 1
        o = _Op(self.dma_eng[ch], emit, True, tag)
        o.sem = ent[0]
        o.val = ent[1] * 16
        deps = list(deps) + ([ent[2]] if ent[2] is not None else [])
        ent[2] = o
        self._wire(o, reads, writes, deps)
        self.seq[o.eng].append(o)
        if ch == "ST":
            self.last_store[o.sem] = o
        return o

    def _wire(self, o, reads, writes, deps):
        dd = {}
        for d in deps:
            if d is not None:
                dd[id(d)] = d
        for r in reads:
            if r.last_w is not None:
                dd[id(r.last_w)] = r.last_w
            r.readers.append(o)
        for w in writes:
            if w.last_w is not None:
                dd[id(w.last_w)] = w.last_w
            for d in w.readers:
                dd[id(d)] = d
            w.last_w = o
            w.readers = []
        o.deps = [d for d in dd.values() if d is not o]

    def sem_names(self):
        return ["E_" + e for e in ENGS] + [v[0] for v in self.dma_sems.values()]

    def emit_engine(self, engine_obj, eng, sems):
        watermark = {}
        for o in self.seq[eng]:
            needs = {}
            for d in o.deps:
                if eng == "tensor" and d.eng == "tensor" and not d.is_dma:
                    continue
                if needs.get(d.sem, -1) < d.val:
                    needs[d.sem] = d.val
            for k in sorted(needs):
                v = needs[k]
                if watermark.get(k, -1) >= v:
                    continue
                engine_obj.wait_ge(sems[k], v)
                watermark[k] = v
            inst = o.emit(engine_obj)
            if inst is None:
                raise RuntimeError("emit None: %s" % o.tag)
            inst.then_inc(sems[o.sem], 16 if o.is_dma else 1)

    def check(self):
        idx = {e: 0 for e in ENGS}
        semv = {}
        total = sum(len(self.seq[e]) for e in ENGS)
        done = 0
        progress = True
        while progress:
            progress = False
            for e in ENGS:
                while idx[e] < len(self.seq[e]):
                    o = self.seq[e][idx[e]]
                    if not all(semv.get(d.sem, 0) >= d.val for d in o.deps):
                        break
                    semv[o.sem] = max(semv.get(o.sem, 0), o.val)
                    idx[e] += 1
                    done += 1
                    progress = True
        if done != total:
            for e in ENGS:
                if idx[e] < len(self.seq[e]):
                    o = self.seq[e][idx[e]]
                    missing = [(d.sem, d.val, semv.get(d.sem, 0), d.tag)
                               for d in o.deps if semv.get(d.sem, 0) < d.val]
                    raise RuntimeError("deadlock %s[%d] %s missing=%s"
                                       % (e, idx[e], o.tag, missing[:4]))


def _rope_tables():
    inv = THETA ** (-np.arange(64, dtype=np.float64) / 64.0)
    f = np.arange(S, dtype=np.float64)[None, :] * inv[:, None]
    c = np.cos(f)
    s = np.sin(f)
    tabc = np.concatenate([c, c], 0).astype(np.float32)
    tabs = np.concatenate([-s, s], 0).astype(np.float32)
    return tabc, tabs


def _masks_a():
    k = np.arange(128)[:, None]
    q = np.arange(512)[None, :]
    return np.stack([(q >= 128 * j + k) for j in range(4)]).astype(np.float32)


def _masks_b():
    k = np.arange(128)[:, None]
    q = np.arange(256)[None, :]
    out = []
    for j in (0, 1, 4, 5):
        m = ((q - k) <= (128 * j - 1)) & ((q - k) >= (128 * j - 512))
        out.append(m.astype(np.float32))
    return np.stack(out)


def _tile_w(w):
    K, M = w.shape
    return np.ascontiguousarray(w.reshape(K // 128, 128, M))


def _head_slab(w):
    K, M = w.shape
    return np.ascontiguousarray(
        w.reshape(K // 128, 128, M // 128, 128).transpose(2, 0, 1, 3))


def _quarter_slab(w):
    K, M = w.shape
    return np.ascontiguousarray(
        w.reshape(K // 128, 128, M // 512, 512).transpose(2, 0, 1, 3))


def host_arrays(inputs):
    A = {}
    A["wq_a"] = _head_slab(inputs["L0_attn_a_Wq"])
    A["wk_a"] = _head_slab(inputs["L0_attn_a_Wk"])
    A["wv_a"] = _tile_w(inputs["L0_attn_a_Wv"])
    A["wo_a"] = _quarter_slab(inputs["L0_attn_a_Wo"])
    A["wq_b"] = _head_slab(np.ascontiguousarray(inputs["L0_attn_b_Wq"].T))
    A["wk_b"] = _head_slab(np.ascontiguousarray(inputs["L0_attn_b_Wk"].T))
    A["wv_b"] = _tile_w(np.ascontiguousarray(inputs["L0_attn_b_Wv"].T))
    A["wo_b"] = _quarter_slab(np.ascontiguousarray(inputs["L0_attn_b_Wo"].T))
    A["gain"] = np.ascontiguousarray(
        inputs["L0_attn_b_q_gain"].reshape(1, NH).astype(np.float32))
    A["w_ffn"] = _tile_w(np.ascontiguousarray(inputs["L0_ffn_weight"].T))
    la = inputs["L0_ffn_lora_A"].reshape(NE * RANK, DM)
    A["la"] = _tile_w(np.ascontiguousarray(la.T))
    lb = inputs["L0_ffn_lora_B"].transpose(0, 2, 1).reshape(NE * RANK, DM)
    A["lb"] = np.ascontiguousarray(lb)
    A["rw"] = _tile_w(np.ascontiguousarray(inputs["L0_ffn_router_w"].T))
    A["ident"] = np.eye(128, dtype=np.float32)
    A["swp"] = np.roll(np.eye(128, dtype=np.float32), 64, axis=1)
    A["tabc"], A["tabs"] = _rope_tables()
    A["mask_a"] = _masks_a()
    A["mask_b"] = _masks_b()
    A["rep"] = np.ascontiguousarray(
        np.repeat(np.eye(NE, dtype=np.float32), RANK, axis=1) * SCALING)
    fc = np.array([EPS_LN, EPS_RMS, math.log(ISD), 0.0], dtype=np.float32)
    A["fconst"] = np.ascontiguousarray(np.broadcast_to(fc, (128, 4)).copy())
    return A


SUBLAYERS = ("attn_a", "ffn", "attn_b", "ffn")


def build_nc(sublayers=(0, 1, 2, 3), nb=NB):
    nc = bass.Bass("TRN2", target_bir_lowering=False)
    P = Prog()
    P.channel("W", "gpsimd")
    P.channel("XA", "sync")
    P.channel("XD", "sync")
    P.channel("ST", "gpsimd")

    last_sub = sublayers[-1]
    first_sub = sublayers[0]

    xin = nc.dram_tensor("xin", [nb, S, DM], F32, kind="ExternalInput").ap()
    xout = nc.dram_tensor("xout", [nb, S, DM], F32, kind="ExternalOutput").ap()
    xres = nc.dram_tensor("xres", [nb, S, DM], F32, kind="Internal").ap()
    dr = {}
    for name, shape in [
        ("wq_a", [NH, KT, 128, 128]), ("wk_a", [NKV, KT, 128, 128]),
        ("wv_a", [KT, 128, 512]), ("wo_a", [4, KT, 128, 512]),
        ("wq_b", [NH, KT, 128, 128]), ("wk_b", [NKV, KT, 128, 128]),
        ("wv_b", [KT, 128, 512]), ("wo_b", [4, KT, 128, 512]),
        ("gain", [1, NH]), ("w_ffn", [KT, 128, DM]), ("la", [KT, 128, 128]),
        ("lb", [128, DM]), ("rw", [KT, 128, NE]),
        ("ident", [128, 128]), ("swp", [128, 128]),
        ("tabc", [128, S]), ("tabs", [128, S]),
        ("mask_a", [4, 128, 512]), ("mask_b", [4, 128, 256]), ("rep", [NE, 128]),
        ("fconst", [128, 4]),
    ]:
        dr[name] = nc.dram_tensor(name, shape, F32, kind="ExternalInput").ap()

    st = ExitStack()
    E = st.enter_context

    ident = E(nc.sbuf_tensor("c_ident", [128, 128], BF16))
    ones_col = E(nc.sbuf_tensor("c_ones_col", [128, 1], BF16))
    ones_row = E(nc.sbuf_tensor("c_ones_row", [1, 128], F32))
    swp_c = E(nc.sbuf_tensor("c_swp", [128, 128], BF16))
    tabc = E(nc.sbuf_tensor("c_tabc", [128, S], BF16))
    tabs = E(nc.sbuf_tensor("c_tabs", [128, S], BF16))
    mask_a = E(nc.sbuf_tensor("c_mask_a", [128, 4, 512], BF16))
    mask_b = E(nc.sbuf_tensor("c_mask_b", [128, 4, 256], BF16))
    rep = E(nc.sbuf_tensor("c_rep", [NE, 128], BF16))
    gain = E(nc.sbuf_tensor("c_gain", [1, NH], F32))
    fc_sb = E(nc.sbuf_tensor("c_fconst", [128, 4], F32))

    big = E(nc.sbuf_tensor("big", [128, 40960], BF16))
    khat = big[:, 0:8192].rearrange("p (a c) -> p a c", a=NKV)
    vsb = big[:, 8192:16384].rearrange("p (a c) -> p a c", a=NT)
    qhat = big[:, 16384:32768].rearrange("p (a c) -> p a c", a=NH)
    ptv = big[:, 32768:40960]
    pt = ptv.rearrange("p (a c) -> p a c", a=16)
    wffn = big[:, 0:32768].rearrange("p (a c) -> p a c", a=KT)
    la_sb = big[:, 32768:34816].rearrange("p (a c) -> p a c", a=KT)
    lb_sb = big[:, 34816:36864]
    rw_sb = big[:, 36864:36992].rearrange("p (a c) -> p a c", a=KT)
    axt = big[:, 36992:38016]
    gt_sb = big[:, 38016:39040]
    wv_full = big[:, 32768:40960].rearrange("p (k c) -> p k c", k=KT)
    wo_slab = [big[:, 16384:24576].rearrange("p (k c) -> p k c", k=KT),
               big[:, 24576:32768].rearrange("p (k c) -> p k c", k=KT),
               big[:, 32768:40960].rearrange("p (k c) -> p k c", k=KT),
               None]

    xlnT = E(nc.sbuf_tensor("xlnT", [128, KT, 1024], BF16))
    yT = xlnT
    yflat = xlnT[:, :, :].rearrange("p a c -> p (a c)")

    wsl_ar = E(nc.sbuf_tensor("wsl_ar", [128, 8192], BF16))
    wq_slab = wsl_ar[:, 0:4096].rearrange("p (s k c) -> p s k c", s=2, k=KT)
    wk_slab = wsl_ar[:, 4096:8192].rearrange("p (s k c) -> p s k c", s=2, k=KT)
    wo_sl0 = wsl_ar[:, 0:8192].rearrange("p (k c) -> p k c", k=KT)
    wv_slab = E(nc.sbuf_tensor("wv_slab", [128, 2, 512], BF16))
    wo_slab[3] = wo_sl0

    xar = E(nc.sbuf_tensor("xar", [128, 8192], F32))
    xf_buf = [xar[:, 0:2048], xar[:, 2048:4096]]
    xd_buf = [xar[:, 4096:5120], xar[:, 5120:6144]]
    xn_buf = [xar[:, 6144:7168], xar[:, 7168:8192]]
    xnf_buf = [xar[:, 4096:6144], xar[:, 6144:8192]]

    xln_bf = [E(nc.sbuf_tensor("xlnb%d" % i, [128, DM], BF16)) for i in range(2)]
    sq_sb = E(nc.sbuf_tensor("sq_sb", [128, 1024], BF16))
    _rawbf = E(nc.sbuf_tensor("rawbf", [128, 1024], BF16))
    raw_bf = [_rawbf, _rawbf]
    swp_bf = E(nc.sbuf_tensor("swpbf", [128, 1024], BF16))
    rope_t2 = E(nc.sbuf_tensor("rope_t2", [128, 1024], BF16))
    rope_tmp = E(nc.sbuf_tensor("rope_tmp", [128, 1024], BF16))
    rope_o = E(nc.sbuf_tensor("rope_o", [128, 1024], BF16))
    stats = [E(nc.sbuf_tensor("stats%d" % i, [128, 8], F32)) for i in range(2)]
    stats8 = E(nc.sbuf_tensor("stats8", [128, 256], F32))
    rstd = [E(nc.sbuf_tensor("rstd%d" % i, [128, 1], F32)) for i in range(2)]
    _trow = E(nc.sbuf_tensor("trow", [1, 1024], F32))
    trow = [_trow, _trow]
    _rc = E(nc.sbuf_tensor("rc", [1, 512], F32))
    _rbc = E(nc.sbuf_tensor("rbc", [128, 512], F32))
    rc_sb = [_rc, _trow[0:1, 0:512]]
    rbc = [_rbc, rope_tmp[:, :].bitcast(F32)[:, 0:512]]
    bias_sb = E(nc.sbuf_tensor("bias_sb", [128, 16], F32))
    bnd = E(nc.sbuf_tensor("bnd", [1, 8], F32))
    qn2 = E(nc.sbuf_tensor("qn2", [1, NH], F32))
    kn2 = E(nc.sbuf_tensor("kn2", [1, NKV], F32))
    wdt = rope_o
    r_m1s = E(nc.sbuf_tensor("r_m1s", [128, 1], F32))
    r_m2s = E(nc.sbuf_tensor("r_m2s", [128, 1], F32))
    r_l2s = E(nc.sbuf_tensor("r_l2s", [128, NE], F32))
    r_k1s = E(nc.sbuf_tensor("r_k1s", [128, NE], F32))
    r_k2s = E(nc.sbuf_tensor("r_k2s", [128, NE], F32))
    r_e2s = E(nc.sbuf_tensor("r_e2s", [128, 1], F32))
    r_s2s = E(nc.sbuf_tensor("r_s2s", [128, 1], F32))
    r_w1s = E(nc.sbuf_tensor("r_w1s", [128, 1], F32))
    r_w2s = E(nc.sbuf_tensor("r_w2s", [128, 1], F32))
    r_wds = E(nc.sbuf_tensor("r_wds", [128, NE], BF16))
    r_t8 = E(nc.sbuf_tensor("r_t8", [128, NE], F32))
    wds8 = E(nc.sbuf_tensor("wds8", [128, 64], BF16))

    psA = E(nc.psum_tensor("psA", [128, 2048], F32))
    psB = E(nc.psum_tensor("psB", [128, 2048], F32))
    psAv = [psA[:, 0:1024], psA[:, 1024:2048]]
    psBs = [psB[:, 0:512], psB[:, 512:1024], psB[:, 1024:1536], psB[:, 1536:2048]]
    tr_banks = [psA[:, i * 512:(i + 1) * 512].bitcast(BF16) for i in range(4)] + \
               [psB[:, i * 512:(i + 1) * 512].bitcast(BF16) for i in range(4)]

    r_xf = [[Res("xf0a"), Res("xf0b")], [Res("xf1a"), Res("xf1b")]]
    r_xd = [Res("xd0"), Res("xd1")]
    r_xn = [Res("xn0"), Res("xn1")]
    r_xlnb = [[Res("xlnb0a"), Res("xlnb0b")], [Res("xlnb1a"), Res("xlnb1b")]]
    r_stats = [Res("st0"), Res("st1")]
    r_st8 = [Res("st8_%d" % i) for i in range(NT)]
    r_rstd = [Res("rs0"), Res("rs1")]
    r_xlnT = Res("xlnT")
    r_qhat = Res("qhat")
    r_pt = Res("pt")
    r_ptg = [Res("ptg%d" % i) for i in range(8)]
    r_khat = Res("khat")
    r_vsb = Res("vsb")
    r_wq = [Res("wq0"), Res("wq1")]
    r_wk = [Res("wk0"), Res("wk1")]
    r_wv = [Res("wv0"), Res("wv1")]
    r_wffn = Res("wffn")
    r_lora = Res("lora")
    r_consts = Res("consts")
    r_sq = Res("sq")
    _rraw = Res("raw")
    r_raw = [_rraw, _rraw]
    r_swpb = Res("swpb")
    r_ropet = Res("ropet")
    r_ropet2 = Res("ropet2")
    r_ropeo = Res("ropeo")
    _rtrow = Res("trow")
    r_trow = [_rtrow, _rtrow]
    r_rc = [Res("rc"), _rtrow]
    r_rbc = [Res("rbc"), r_ropet]
    r_bias = Res("bias")
    r_biasv = [Res("bias%d" % i) for i in range(16)]
    r_bnd = Res("bnd")
    r_qn = Res("qn")
    r_kn = Res("kn")
    r_rt = Res("router")
    r_gt = Res("gt")
    r_axt = Res("axt")
    r_wdt = Res("wdt")
    r_wds8 = [Res("wds8_%d" % i) for i in range(8)]
    r_psA = [Res("psA0"), Res("psA1")]
    r_psB = [Res("psB0"), Res("psB1"), Res("psB2"), Res("psB3")]
    r_hbm = [[Res("hbm_%d_%d" % (b, t)) for t in range(NT)] for b in range(nb)]

    r_trb = [r_psA[0], r_psA[0], r_psA[1], r_psA[1],
             r_psB[0], r_psB[1], r_psB[2], r_psB[3]]
    ALL_BIG_ATTN = [r_khat, r_vsb, r_qhat, r_pt]

    def arena_handoff(olds, news):
        pend = []
        for r in olds:
            if r.last_w is not None:
                pend.append(r.last_w)
            pend.extend(r.readers)
        seen = {id(p): p for p in pend}
        for r in news:
            have = {id(x) for x in r.readers}
            for p in seen.values():
                if id(p) not in have:
                    r.readers.append(p)

    P.dma("W", "cons", lambda g: g.dma_start(ident[:, :], dr["ident"]),
          writes=[r_consts], tag="ld_ident")
    P.dma("W", "cons", lambda g: g.dma_start(swp_c[:, :], dr["swp"]),
          writes=[r_consts], tag="ld_swp")
    P.dma("W", "cons", lambda g: g.dma_start(tabc[:, :], dr["tabc"]),
          writes=[r_consts], tag="ld_tabc")
    P.dma("W", "cons", lambda g: g.dma_start(tabs[:, :], dr["tabs"]),
          writes=[r_consts], tag="ld_tabs")
    P.dma("W", "cons", lambda g: g.dma_start(
        mask_a[:, :, :], dr["mask_a"].rearrange("j p q -> p j q")),
        writes=[r_consts], tag="ld_ma")
    P.dma("W", "cons", lambda g: g.dma_start(
        mask_b[:, :, :], dr["mask_b"].rearrange("j p q -> p j q")),
        writes=[r_consts], tag="ld_mb")
    P.dma("W", "cons", lambda g: g.dma_start(rep[:, :], dr["rep"]),
          writes=[r_consts], tag="ld_rep")
    P.dma("W", "cons", lambda g: g.dma_start(gain[:, :], dr["gain"]),
          writes=[r_consts], tag="ld_gain")
    P.dma("W", "cons", lambda g: g.dma_start(fc_sb[:, :], dr["fconst"]),
          writes=[r_consts], tag="ld_fc")
    P.op("vector", lambda v: v.memset(ones_col[:, :], 1.0), writes=[r_consts], tag="ones_c")
    P.op("vector", lambda v: v.memset(ones_row[:, :], 1.0), writes=[r_consts], tag="ones_r")

    def ld_wq_head(kind, h, slot):
        nm = "wq_a" if kind == "attn_a" else "wq_b"
        return P.dma("W", "wq%d" % slot, lambda g, h=h, slot=slot, nm=nm: g.dma_start(
            wq_slab[:, slot, :, :], dr[nm][h].rearrange("k p c -> p k c")),
            writes=[r_wq[slot]], tag="ld_wq%d" % h)

    def ld_wk_head(kind, kv, slot):
        nm = "wk_a" if kind == "attn_a" else "wk_b"
        return P.dma("W", "wk%d" % slot, lambda g, kv=kv, slot=slot, nm=nm: g.dma_start(
            wk_slab[:, slot, :, :], dr[nm][kv].rearrange("k p c -> p k c")),
            writes=[r_wk[slot]], tag="ld_wk%d" % kv)

    def ld_wv_full(kind):
        nm = "wv_a" if kind == "attn_a" else "wv_b"
        return P.dma("W", "wvf", lambda g, nm=nm: g.dma_start(
            wv_full[:, :, :], dr[nm].rearrange("k p c -> p k c")),
            writes=[r_pt] + r_ptg, tag="ld_wvf")

    def ld_wv_tile(kind, k, slot):
        nm = "wv_a" if kind == "attn_a" else "wv_b"
        return P.dma("W", "wv%d" % slot, lambda g, k=k, slot=slot, nm=nm: g.dma_start(
            wv_slab[:, slot, :], dr[nm][k]),
            writes=[r_wv[slot]], tag="ld_wv%d" % k)

    def ld_ffn_weights():
        P.dma("W", "ffw", lambda g: g.dma_start(
            wffn[:, :, :], dr["w_ffn"].rearrange("k p c -> p k c")),
            writes=[r_wffn] + ALL_BIG_ATTN[:3], tag="ld_wffn")
        P.dma("W", "ffl", lambda g: g.dma_start(
            la_sb[:, :, :], dr["la"].rearrange("k p c -> p k c")),
            writes=[r_lora, r_pt] + r_ptg, tag="ld_la")
        P.dma("W", "ffl2", lambda g: g.dma_start(lb_sb[:, :], dr["lb"]),
              writes=[r_lora, r_pt], tag="ld_lb")
        P.dma("W", "ffl3", lambda g: g.dma_start(
            rw_sb[:, :, :], dr["rw"].rearrange("k p c -> p k c")),
            writes=[r_lora, r_pt], tag="ld_rw")

    def ld_wo_quarter(kind, q, buf_idx):
        nm = "wo_a" if kind == "attn_a" else "wo_b"
        rr = [[r_qhat], [r_qhat], [r_pt] + r_ptg,
              [r_wq[0], r_wq[1], r_wk[0], r_wk[1]]][buf_idx]
        return P.dma("W", "wo%d" % buf_idx, lambda g, q=q, buf_idx=buf_idx, nm=nm: g.dma_start(
            wo_slab[buf_idx][:, :, :], dr[nm][q].rearrange("k p c -> p k c")),
            writes=rr, tag="ld_wo%d" % q)


    def _stats_tail(t, c, rr):
        P.op("vector", lambda v: v.tensor_scalar(
            stats8[:, c + 8:c + 9], stats8[:, c + 0:c + 1], 1.0 / DM, None, ALU.mult),
            reads=rr, writes=rr, tag="SCm%d" % t)
        P.op("vector", lambda v: v.tensor_scalar(
            stats8[:, c + 9:c + 10], stats8[:, c + 8:c + 9], -1.0, None, ALU.mult),
            reads=rr, writes=rr, tag="SCn%d" % t)
        P.op("vector", lambda v: v.tensor_scalar(
            stats8[:, c + 11:c + 12], stats8[:, c + 2:c + 3], 1.0 / DM, None, ALU.mult),
            reads=rr, writes=rr, tag="SCe%d" % t)
        P.op("vector", lambda v: v.scalar_tensor_tensor(
            stats8[:, c + 9:c + 10], stats8[:, c + 9:c + 10], stats8[:, c + 8:c + 9],
            stats8[:, c + 11:c + 12], ALU.mult, ALU.add),
            reads=rr, writes=rr, tag="SCv%d" % t)
        P.op("scalar", lambda s_: s_.activation(
            stats8[:, c + 9:c + 10], stats8[:, c + 9:c + 10], AF.Ln, bias=fc_sb[:, 0:1]),
            reads=rr + [r_consts], writes=rr, tag="SCl%d" % t)
        P.op("scalar", lambda s_: s_.activation(
            stats8[:, c + 10:c + 11], stats8[:, c + 9:c + 10], AF.Exp, scale=-0.5),
            reads=rr, writes=rr, tag="SCr%d" % t)

    def stats_chain2(t):
        c = t * 16
        rr = [r_st8[t]]
        P.op("vector", lambda v: v.tensor_add(
            stats8[:, c + 0:c + 1], stats8[:, c + 0:c + 1], stats8[:, c + 1:c + 2]),
            reads=rr, writes=rr, tag="SCa%d" % t)
        P.op("vector", lambda v: v.tensor_add(
            stats8[:, c + 2:c + 3], stats8[:, c + 2:c + 3], stats8[:, c + 3:c + 4]),
            reads=rr, writes=rr, tag="SCd%d" % t)
        _stats_tail(t, c, rr)

    def stats_chain4(t):
        c = t * 16
        rr = [r_st8[t]]
        P.op("vector", lambda v: v.tensor_add(
            stats8[:, c + 0:c + 1], stats8[:, c + 0:c + 1], stats8[:, c + 1:c + 2]),
            reads=rr, writes=rr, tag="SCa%d" % t)
        P.op("vector", lambda v: v.tensor_add(
            stats8[:, c + 4:c + 5], stats8[:, c + 4:c + 5], stats8[:, c + 5:c + 6]),
            reads=rr, writes=rr, tag="SCb%d" % t)
        P.op("vector", lambda v: v.tensor_add(
            stats8[:, c + 0:c + 1], stats8[:, c + 0:c + 1], stats8[:, c + 4:c + 5]),
            reads=rr, writes=rr, tag="SCc%d" % t)
        P.op("vector", lambda v: v.tensor_add(
            stats8[:, c + 2:c + 3], stats8[:, c + 2:c + 3], stats8[:, c + 3:c + 4]),
            reads=rr, writes=rr, tag="SCd%d" % t)
        P.op("vector", lambda v: v.tensor_add(
            stats8[:, c + 6:c + 7], stats8[:, c + 6:c + 7], stats8[:, c + 7:c + 8]),
            reads=rr, writes=rr, tag="SCf%d" % t)
        P.op("vector", lambda v: v.tensor_add(
            stats8[:, c + 2:c + 3], stats8[:, c + 2:c + 3], stats8[:, c + 6:c + 7]),
            reads=rr, writes=rr, tag="SCg%d" % t)
        _stats_tail(t, c, rr)

    def stats_chain1(t):
        _stats_tail(t, t * 16, [r_st8[t]])

    def phase_a(b, s, H, vkind=None, tile_cb=None):
        if s != first_sub:
            return phase_a_apply(b, s, H, vkind=vkind, tile_cb=tile_cb)
        xsrc = xin[b]

        def lda(tl):
            t = H * 8 + tl
            j = tl % 4
            P.dma("XA", j, lambda sy, t=t, j=j: sy.dma_start(
                a_stage[j][:, 0:1024], xsrc[t * 128:(t + 1) * 128, 0:1024]),
                reads=[r_hbm[b][t]], writes=a_stage_hres[j][0], tag="Ald%d" % t)
            P.dma("XD", j, lambda sy, t=t, j=j: sy.dma_start(
                a_stage[j][:, 1024:2048], xsrc[t * 128:(t + 1) * 128, 1024:2048]),
                reads=[r_hbm[b][t]], writes=a_stage_hres[j][1], tag="Ald2%d" % t)

        for tl in range(4):
            lda(tl)
        for tl in range(8):
            t = H * 8 + tl
            sl = tl % 2
            j = tl % 4
            P.op("vector", lambda v, sl=sl, j=j: v.tensor_scalar(
                xln_bf[sl][:, :], a_stage[j], 1.0, 0.0, ALU.mult, ALU.add,
                accum_out=stats[sl][:, 0:1]),
                reads=a_stage_res[j], writes=r_xlnb[sl] + [r_stats[sl]], tag="Asum%d" % t)
            P.op("scalar", lambda s_, sl=sl, j=j: s_.activation(
                psB[:, :], a_stage[j], AF.Square, accum_out=stats[sl][:, 1:2]),
                reads=a_stage_res[j],
                writes=[r_psB[0], r_psB[1], r_psB[2], r_psB[3], r_stats[sl]],
                tag="Asq%d" % t)
            P.op("vector", lambda v, sl=sl: v.tensor_scalar(
                stats[sl][:, 2:3], stats[sl][:, 0:1], 1.0 / DM, None, ALU.mult),
                reads=[r_stats[sl]], writes=[r_stats[sl]], tag="Am%d" % t)
            P.op("vector", lambda v, sl=sl: v.tensor_scalar(
                stats[sl][:, 3:4], stats[sl][:, 1:2], 1.0 / DM, None, ALU.mult),
                reads=[r_stats[sl]], writes=[r_stats[sl]], tag="Ams%d" % t)
            P.op("vector", lambda v, sl=sl: v.tensor_scalar(
                stats[sl][:, 4:5], stats[sl][:, 2:3], -1.0, None, ALU.mult),
                reads=[r_stats[sl]], writes=[r_stats[sl]], tag="Anm%d" % t)
            P.op("vector", lambda v, sl=sl: v.scalar_tensor_tensor(
                stats[sl][:, 5:6], stats[sl][:, 4:5], stats[sl][:, 2:3],
                stats[sl][:, 3:4], ALU.mult, ALU.add),
                reads=[r_stats[sl]], writes=[r_stats[sl]], tag="Avar%d" % t)
            P.op("scalar", lambda s_, sl=sl: s_.activation(
                stats[sl][:, 6:7], stats[sl][:, 5:6], AF.Ln, bias=fc_sb[:, 0:1]),
                reads=[r_stats[sl], r_consts], writes=[r_stats[sl]], tag="Aln%d" % t)
            P.op("scalar", lambda s_, sl=sl: s_.activation(
                rstd[sl][:, 0:1], stats[sl][:, 6:7], AF.Exp, scale=-0.5),
                reads=[r_stats[sl]], writes=[r_rstd[sl]], tag="Ar%d" % t)
            P.op("vector", lambda v, sl=sl, j=j: v.tensor_scalar(
                xln_bf[sl][:, :], a_stage[j], stats[sl][:, 2:3],
                rstd[sl][:, 0:1], ALU.subtract, ALU.mult),
                reads=a_stage_res[j] + [r_stats[sl], r_rstd[sl]], writes=r_xlnb[sl],
                tag="Aap%d" % t)
            if tl + 4 < 8:
                lda(tl + 4)
            for half in range(2):
                bs = (2 * tl + half) % 4
                pr = r_trb[bs]
                dstv = tr_banks[bs]
                for j in range(8):
                    k = half * 8 + j
                    P.op("tensor", lambda t_, sl=sl, dstv=dstv, j=j, k=k: t_.transpose(
                        dstv[:, j * 128:(j + 1) * 128],
                        xln_bf[sl][:, k * 128:(k + 1) * 128], ident[:, :]),
                        reads=r_xlnb[sl] + [r_consts], writes=[pr],
                        tag="Atr%d_%d" % (t, k))
                dstx = xlnT[:, half * 8:half * 8 + 8, tl * 128:tl * 128 + 128]
                srcx = dstv[:, 0:1024].rearrange("p (a c) -> p a c", a=8)
                if half == 0:
                    P.op("vector", lambda v, dstx=dstx, srcx=srcx: v.tensor_copy(dstx, srcx),
                         reads=[pr], writes=[r_xlnT], tag="Acp%d_0" % t)
                else:
                    P.op("scalar", lambda s_, dstx=dstx, srcx=srcx: s_.copy(dstx, srcx),
                         reads=[pr], writes=[r_xlnT], tag="Acp%d_1" % t)


    a_stage = [xf_buf[0], xf_buf[1], xnf_buf[0], xnf_buf[1]]
    a_stage_res = [r_xf[0], r_xf[1], [r_xd[0], r_xd[1]], [r_xn[0], r_xn[1]]]
    a_stage_hres = [[[r_xf[0][0]], [r_xf[0][1]]],
                    [[r_xf[1][0]], [r_xf[1][1]]],
                    [[r_xd[0]], [r_xd[1]]],
                    [[r_xn[0]], [r_xn[1]]]]

    def phase_a_apply(b, s, H, vkind=None, tile_cb=None):
        xsrc = xres[b]
        if vkind is not None:
            ld_wv_full(vkind)

        def ld(tl):
            t = H * 8 + tl
            j = tl % 4
            P.dma("XA", j, lambda sy, t=t, j=j: sy.dma_start(
                a_stage[j][:, 0:1024], xsrc[t * 128:(t + 1) * 128, 0:1024]),
                reads=[r_hbm[b][t]], writes=a_stage_hres[j][0], tag="ALd%d" % t)
            P.dma("XD", j, lambda sy, t=t, j=j: sy.dma_start(
                a_stage[j][:, 1024:2048], xsrc[t * 128:(t + 1) * 128, 1024:2048]),
                reads=[r_hbm[b][t]], writes=a_stage_hres[j][1], tag="ALd2%d" % t)

        for tl in range(4):
            ld(tl)
        for tl in range(8):
            t = H * 8 + tl
            sl = tl % 2
            j = tl % 4
            cc = t * 16
            nslot = 6 if vkind is not None else (5 if tile_cb is not None else 8)
            for half in range(2):
                P.op("vector", lambda v, sl=sl, cc=cc, j=j, half=half: v.tensor_scalar(
                    xln_bf[sl][:, half * 1024:(half + 1) * 1024],
                    a_stage[j][:, half * 1024:(half + 1) * 1024],
                    stats8[:, cc + 8:cc + 9],
                    stats8[:, cc + 10:cc + 11], ALU.subtract, ALU.mult),
                    reads=a_stage_hres[j][half] + [r_st8[t]],
                    writes=[r_xlnb[sl][half]], tag="AAp%d_%d" % (t, half))
                if half == 1 and tl + 4 < 8:
                    ld(tl + 4)
                bs = (2 * tl + half) % nslot
                pr = r_trb[bs]
                dstv = tr_banks[bs]
                for j2 in range(8):
                    k = half * 8 + j2
                    P.op("tensor", lambda t_, sl=sl, dstv=dstv, j2=j2, k=k, half=half: t_.transpose(
                        dstv[:, j2 * 128:(j2 + 1) * 128],
                        xln_bf[sl][:, k * 128:(k + 1) * 128], ident[:, :]),
                        reads=[r_xlnb[sl][half], r_consts], writes=[pr],
                        tag="AAtr%d_%d" % (t, k))
                dstx = xlnT[:, half * 8:half * 8 + 8, tl * 128:tl * 128 + 128]
                srcx = dstv[:, 0:1024].rearrange("p (a c) -> p a c", a=8)
                if half == 0:
                    P.op("vector", lambda v, dstx=dstx, srcx=srcx: v.tensor_copy(dstx, srcx),
                         reads=[pr], writes=[r_xlnT], tag="AAcp%d_0" % t)
                else:
                    P.op("scalar", lambda s_, dstx=dstx, srcx=srcx: s_.copy(dstx, srcx),
                         reads=[pr], writes=[r_xlnT], tag="AAcp%d_1" % t)
            if vkind is not None:
                vb = 6 + (tl % 2)
                vpr = r_trb[vb]
                for k in range(KT):
                    P.op("tensor", lambda t_, k=k, vb=vb, tl=tl: t_.matmul(
                        psB[:, (vb - 4) * 512:(vb - 3) * 512],
                        xlnT[:, k, tl * 128:(tl + 1) * 128], wv_full[:, k, :],
                        start=(k == 0), stop=(k == KT - 1)),
                        reads=[r_xlnT, r_pt], writes=[vpr], tag="AV%d_%d" % (t, k))
                P.op("scalar", lambda s_, vb=vb, t=t: s_.copy(
                    vsb[:, t, :], psB[:, (vb - 4) * 512:(vb - 3) * 512]),
                    reads=[vpr], writes=[r_vsb], tag="AVc%d" % t)
            if tile_cb is not None:
                tile_cb(tl)

    def rope_ops(dst, dst_res, sl, toff, sign_b, tag, split=False):
        src = raw_bf[sl]
        src_res = r_raw[sl]
        pr = r_psA[sl]
        for c in range(2):
            P.op("tensor", lambda t_, c=c, sl=sl: t_.matmul(
                psA[:, sl * 1024 + c * 512: sl * 1024 + (c + 1) * 512],
                swp_c[:, :], src[:, c * 512:(c + 1) * 512],
                start=True, stop=True),
                reads=[src_res, r_consts], writes=[pr], tag=tag + "sw%d" % c)
        P.op("scalar", lambda s_, sl=sl: s_.copy(swp_bf[:, :], psAv[sl]),
             reads=[pr], writes=[r_swpb], tag=tag + "se")
        P.op("vector", lambda v: v.tensor_mul(
            rope_tmp[:, :], src[:, :], tabc[:, toff:toff + 1024]),
            reads=[src_res, r_consts], writes=[r_ropet], tag=tag + "m1")
        P.op("vector", lambda v: v.tensor_mul(
            rope_t2[:, :], swp_bf[:, :], tabs[:, toff:toff + 1024]),
            reads=[r_swpb, r_consts], writes=[r_ropet2], tag=tag + "m2")

        def post():
            P.op("vector", lambda v: v.tensor_tensor(
                dst, rope_tmp[:, :], rope_t2[:, :],
                ALU.subtract if sign_b else ALU.add),
                reads=[r_ropet, r_ropet2], writes=[dst_res], tag=tag + "o")
        if split:
            return post
        post()

    def norm_scale_b_raw(sl, which, idx, tag, split=False):
        bh = [r_psB[2 * sl], r_psB[2 * sl + 1]]
        P.op("scalar", lambda s_, sl=sl: s_.activation(
            sq_sb[:, :], raw_bf[sl][:, :], AF.Square),
            reads=[r_raw[sl]], writes=[r_sq], tag=tag + "sq")
        for c in range(2):
            P.op("tensor", lambda t_, c=c, sl=sl: t_.matmul(
                psB[0:1, sl * 1024 + c * 512: sl * 1024 + (c + 1) * 512],
                ones_col[:, :], sq_sb[:, c * 512:(c + 1) * 512],
                start=True, stop=True),
                reads=[r_sq, r_consts], writes=[bh[c]], tag=tag + "sm%d" % c)
        P.op("vector", lambda v, sl=sl: v.tensor_scalar(
            trow[sl][0:1, :], psB[0:1, sl * 1024:sl * 1024 + 1024],
            1.0 / HD, None, ALU.mult),
            reads=bh, writes=[r_trow[sl]], tag=tag + "msq")
        P.op("scalar", lambda s_, sl=sl: s_.activation(
            trow[sl][0:1, :], trow[sl][0:1, :], AF.Ln, bias=fc_sb[0:1, 1:2]),
            reads=[r_trow[sl], r_consts], writes=[r_trow[sl]], tag=tag + "ln")
        bias_ap = fc_sb[0:1, 2:3] if which == "q" else fc_sb[0:1, 3:4]
        P.op("scalar", lambda s_, sl=sl, bias_ap=bias_ap: s_.activation(
            trow[sl][0:1, :], trow[sl][0:1, :], AF.Exp, scale=-0.5, bias=bias_ap),
            reads=[r_trow[sl], r_consts], writes=[r_trow[sl]], tag=tag + "ex")
        if which == "q":
            P.op("vector", lambda v, sl=sl, idx=idx: v.tensor_scalar(
                trow[sl][0:1, :], trow[sl][0:1, :], gain[0:1, idx:idx + 1],
                None, ALU.mult),
                reads=[r_trow[sl], r_consts], writes=[r_trow[sl]], tag=tag + "g")

        def post():
            for c in range(2):
                P.op("tensor", lambda t_, c=c, sl=sl: t_.matmul(
                    psB[:, sl * 1024 + c * 512: sl * 1024 + (c + 1) * 512],
                    ones_row[:, :], trow[sl][0:1, c * 512:(c + 1) * 512],
                    start=True, stop=True),
                    reads=[r_trow[sl], r_consts], writes=[bh[c]], tag=tag + "bc%d" % c)
        if split:
            return bh, post
        post()
        return bh

    def make_bias(expr_ops, slot, tag, chain=None, defer=None):
        expr_ops()
        nslot = 1 if defer is None else 5 + (slot % 2)
        P.op("vector", lambda v: v.tensor_scalar(
            bnd[0:1, nslot:nslot + 1], bnd[0:1, 0:1], -1.0, None, ALU.mult),
            reads=[r_bnd], writes=[r_bnd], tag=tag + "neg")
        if chain is None:
            bcol, brs = 1536, r_psB[3]
        else:
            bcol, brs = chain * 1024 + 1023, r_psA[chain]

        def post():
            P.op("tensor", lambda t_, bcol=bcol: t_.matmul(
                psA[:, bcol:bcol + 1] if chain is not None else psB[:, bcol:bcol + 1],
                ones_row[:, :], bnd[0:1, nslot:nslot + 1], start=True, stop=True),
                reads=[r_bnd, r_consts], writes=[brs], tag=tag + "bc")
            P.op("vector", lambda v, slot=slot, bcol=bcol: v.tensor_copy(
                bias_sb[:, slot:slot + 1],
                psA[:, bcol:bcol + 1] if chain is not None else psB[:, bcol:bcol + 1]),
                reads=[brs], writes=[r_biasv[slot]], tag=tag + "cp")
        if defer is None:
            post()
        else:
            defer.append(post)

    def phase_b(b, kind, H, do_v=True):
        toff = H * 1024
        is_b = kind == "attn_b"
        if do_v:
            phase_b_v(b, kind, H)
        phase_b_kq_inner(b, kind, H, toff, is_b)

    def phase_b_v(b, kind, H):
        ld_wv_tile(kind, 0, 0)
        ld_wv_tile(kind, 1, 1)
        for k in range(KT):
            wsl = k % 2
            for tl in range(8):
                ps = psA if tl < 4 else psB
                col = (tl % 4) * 512
                pr = r_psA[tl // 2] if tl < 4 else r_psB[tl - 4]
                P.op("tensor", lambda t_, k=k, tl=tl, ps=ps, col=col, wsl=wsl: t_.matmul(
                    ps[:, col:col + 512],
                    xlnT[:, k, tl * 128:(tl + 1) * 128], wv_slab[:, wsl, :],
                    start=(k == 0), stop=(k == KT - 1)),
                    reads=[r_xlnT, r_wv[wsl]], writes=[pr], tag="BV%d_%d" % (k, tl))
            if k + 2 < KT:
                ld_wv_tile(kind, k + 2, wsl)
        for tl in range(8):
            t = H * 8 + tl
            ps = psA if tl < 4 else psB
            col = (tl % 4) * 512
            pr = r_psA[tl // 2] if tl < 4 else r_psB[tl - 4]
            P.op("scalar", lambda s_, ps=ps, col=col, t=t: s_.copy(
                vsb[:, t, :], ps[:, col:col + 512]),
                reads=[pr], writes=[r_vsb], tag="BVc%d" % t)
    def phase_b_kq_inner(b, kind, H, toff, is_b):
        def epilogue_stages(which, idx, sl):
            pr = r_psA[sl]
            dstv = khat[:, idx, toff:toff + 1024] if which == "k" else qhat[:, idx, :]
            dres = r_khat if which == "k" else r_qhat
            st = {}

            def stage1():
                P.op("scalar", lambda s_, sl=sl: s_.copy(raw_bf[sl][:, :], psAv[sl]),
                     reads=[pr], writes=[r_raw[sl]], tag="B%se%d" % (which, idx))
                if is_b:
                    st["bh"], st["np"] = norm_scale_b_raw(
                        sl, "q" if which == "q" else "k", idx,
                        "n%s%d" % (which, idx), split=True)
                    st["rp"] = rope_ops(rope_o[:, :], r_ropeo, sl, toff, True,
                                        "B%sr%d" % (which, idx), split=True)
                else:
                    rope_ops(dstv, dres, sl, toff, False, "B%sr%d" % (which, idx))

            def stage2():
                if is_b:
                    st["np"]()
                    st["rp"]()
                    P.op("vector", lambda v, dstv=dstv, sl=sl: v.tensor_mul(
                        dstv, rope_o[:, :], psB[:, sl * 1024:sl * 1024 + 1024]),
                        reads=[r_ropeo] + st["bh"], writes=[dres],
                        tag="B%ss%d" % (which, idx))
            return stage1, stage2

        heads = [("k", kv) for kv in range(NKV)] + [("q", h) for h in range(NH)]
        ld_wk_head(kind, 0, 0)
        ld_wk_head(kind, 1, 1)
        prev_ep = None
        for hi, (which, idx) in enumerate(heads):
            sl = hi % 2
            pr = r_psA[sl]
            wsl_res = r_wk[sl] if which == "k" else r_wq[sl]
            wsl_view = wk_slab if which == "k" else wq_slab
            for k in range(KT):
                for c in range(2):
                    P.op("tensor", lambda t_, k=k, c=c, sl=sl, wsl_view=wsl_view: t_.matmul(
                        psA[:, sl * 1024 + c * 512: sl * 1024 + (c + 1) * 512],
                        wsl_view[:, sl, k, :], xlnT[:, k, c * 512:(c + 1) * 512],
                        start=(k == 0), stop=(k == KT - 1)),
                        reads=[r_xlnT, wsl_res], writes=[pr],
                        tag="B%s%d_%d_%d" % (which, idx, k, c))
                if k == KT // 2 - 1 and prev_ep is not None:
                    prev_ep[0]()
            if hi + 2 < len(heads):
                w2, i2 = heads[hi + 2]
                if w2 == "k":
                    ld_wk_head(kind, i2, sl)
                else:
                    ld_wq_head(kind, i2, sl)
            if prev_ep is not None:
                prev_ep[1]()
            prev_ep = epilogue_stages(which, idx, sl)
        prev_ep[0]()
        prev_ep[1]()

    def norm_item(which, idx, H):
        toff = H * 1024
        chain = idx % 2
        dstv = khat[:, idx, toff:toff + 1024] if which == "k" else qhat[:, idx, :]
        dres = r_khat if which == "k" else r_qhat
        sqv, sqr = (sq_sb, r_sq) if which == "q" else (rope_tmp, r_ropet)
        co = chain * 1024
        P.op("scalar", lambda s_: s_.activation(sqv[:, :], dstv, AF.Square),
             reads=[dres], writes=[sqr], tag="N%sn%d" % (which, idx))
        for c in range(2):
            P.op("tensor", lambda t_, c=c: t_.matmul(
                psA[0:1, co + c * 512:co + (c + 1) * 512], ones_col[:, :],
                sqv[:, c * 512:(c + 1) * 512], start=True, stop=True),
                reads=[sqr, r_consts], writes=[r_psA[chain]],
                tag="N%snm%d_%d" % (which, idx, c))
        bslot = 2 + chain
        P.op("vector", lambda v: v.tensor_reduce(
            bnd[0:1, bslot:bslot + 1], psA[0:1, co:co + 1024],
            AX.X, ALU.max),
            reads=[r_psA[chain]], writes=[r_bnd],
            tag="N%snr%d" % (which, idx))
        if which == "q":
            P.op("vector", lambda v: v.tensor_copy(
                qn2[0:1, idx:idx + 1], bnd[0:1, bslot:bslot + 1]),
                reads=[r_bnd], writes=[r_qn], tag="Nqnc%d_%d" % (idx, H))
        elif H == 0:
            P.op("vector", lambda v: v.tensor_copy(
                kn2[0:1, idx:idx + 1], bnd[0:1, bslot:bslot + 1]),
                reads=[r_bnd], writes=[r_kn], tag="Nknc%d_%d" % (idx, H))
        else:
            P.op("vector", lambda v: v.tensor_tensor(
                kn2[0:1, idx:idx + 1], kn2[0:1, idx:idx + 1],
                bnd[0:1, bslot:bslot + 1], ALU.max),
                reads=[r_bnd, r_kn], writes=[r_kn], tag="Nknc%d_%d" % (idx, H))

    def attn_bias_a(h):
        def expr():
            kv = h // (NH // NKV)
            P.op("vector", lambda v: v.tensor_tensor(
                bnd[0:1, 2:3], qn2[0:1, h:h + 1], kn2[0:1, kv:kv + 1], ALU.mult),
                reads=[r_qn, r_kn], writes=[r_bnd], tag="bA%dm" % h)
            P.op("scalar", lambda s_: s_.activation(
                bnd[0:1, 3:4], bnd[0:1, 2:3], AF.Ln),
                reads=[r_bnd], writes=[r_bnd], tag="bA%dl" % h)
            P.op("scalar", lambda s_: s_.activation(
                bnd[0:1, 0:1], bnd[0:1, 3:4], AF.Exp, scale=0.5, bias=fc_sb[0:1, 2:3]),
                reads=[r_bnd, r_consts], writes=[r_bnd], tag="bA%de" % h)
        return expr

    def attn_bias_b(h):
        def expr():
            P.op("vector", lambda v: v.tensor_scalar(
                bnd[0:1, 4:5], gain[0:1, h:h + 1], gain[0:1, h:h + 1], 1.0,
                ALU.mult, ALU.add),
                reads=[r_consts], writes=[r_bnd], tag="bB%da" % h)
            P.op("vector", lambda v: v.tensor_scalar(
                bnd[0:1, 0:1], bnd[0:1, 4:5], math.sqrt(HD) / 2.0, None, ALU.mult),
                reads=[r_bnd], writes=[r_bnd], tag="bB%db" % h)
        return expr

    def softmax_tail_steps(h, qc, p2, n, ydst, tag):
        av, sr = r_psB[2 * p2], r_psB[2 * p2 + 1]
        avc = p2 * 1024
        src = p2 * 1024 + 512

        def step1():
            P.op("scalar", lambda s_: s_.activation(
                rc_sb[p2][0:1, 0:n], psB[0:1, src:src + n], AF.Ln),
                reads=[sr], writes=[r_rc[p2]], tag=tag + "rl")
            P.op("scalar", lambda s_: s_.activation(
                rc_sb[p2][0:1, 0:n], rc_sb[p2][0:1, 0:n], AF.Exp, scale=-1.0),
                reads=[r_rc[p2]], writes=[r_rc[p2]], tag=tag + "rc")

        def step2():
            P.op("tensor", lambda t_: t_.matmul(
                psB[:, src:src + n], ones_row[:, :], rc_sb[p2][0:1, 0:n],
                start=True, stop=True),
                reads=[r_rc[p2], r_consts], writes=[sr], tag=tag + "rb")
            P.op("vector", lambda v: v.tensor_copy(
                rbc[p2][:, 0:n], psB[:, src:src + n]),
                reads=[sr], writes=[r_rbc[p2]], tag=tag + "rr")
            P.op("vector", lambda v: v.tensor_mul(
                ydst, psB[:, avc:avc + n], rbc[p2][:, 0:n]),
                reads=[av, r_rbc[p2]], writes=[r_xlnT], tag=tag + "y")
        return step1, step2

    def chunk_gen_a(h, qcl, H, chain):
        kv = h // (NH // NKV)
        bslot = h % 16
        qc = 2 * H + qcl
        av, sr = r_psB[2 * chain], r_psB[2 * chain + 1]
        avc = chain * 1024
        src_ = chain * 1024 + 512
        nkt = 4 * (qc + 1)
        base = chain * 8
        pend_b = []

        def sums_av(kt):
            s = base + (kt % 8)
            g = s // 2
            P.op("tensor", lambda t_, kt=kt, s=s: t_.matmul(
                psB[0:1, src_:src_ + 512], ones_col[:, :], pt[:, s, :],
                start=(kt == 0), stop=(kt == nkt - 1)),
                reads=[r_ptg[g], r_consts], writes=[sr],
                tag="Csm%d_%d_%d" % (h, qc, kt))
            P.op("tensor", lambda t_, kt=kt, s=s: t_.matmul(
                psB[:, avc:avc + 512], vsb[:, kt, kv * 128:(kv + 1) * 128],
                pt[:, s, :], start=(kt == 0), stop=(kt == nkt - 1)),
                reads=[r_ptg[g], r_vsb], writes=[av],
                tag="Cav%d_%d_%d" % (h, qc, kt))

        for kt in range(nkt):
            pcol = chain * 1024 + (kt % 2) * 512
            pg = r_psA[chain]
            s = base + (kt % 8)
            P.op("tensor", lambda t_, kt=kt, pcol=pcol: t_.matmul(
                psA[:, pcol:pcol + 512],
                khat[:, kv, kt * 128:(kt + 1) * 128],
                qhat[:, h, qcl * 512:(qcl + 1) * 512],
                start=True, stop=True),
                reads=[r_khat, r_qhat], writes=[pg],
                tag="CS%d_%d_%d" % (h, qc, kt))
            P.op("scalar", lambda s_, kt=kt, pcol=pcol, s=s: s_.activation(
                pt[:, s, :], psA[:, pcol:pcol + 512],
                AF.Exp, bias=bias_sb[:, bslot:bslot + 1], scale=ISD),
                reads=[pg, r_biasv[bslot]], writes=[r_ptg[s // 2]],
                tag="Ce%d_%d_%d" % (h, qc, kt))
            j = kt - 4 * qc
            if 0 <= j < 4:
                P.op("vector", lambda v, s=s, j=j: v.tensor_mul(
                    pt[:, s, :], pt[:, s, :], mask_a[:, j, :]),
                    reads=[r_ptg[s // 2], r_consts], writes=[r_ptg[s // 2]],
                    tag="Cm%d_%d_%d" % (h, qc, j))
            if kt >= 2:
                sums_av(kt - 2)
            if kt == 2 and qcl == 0 and h + 2 < NH:
                norm_item("q", h + 2, H)
                make_bias(attn_bias_a(h + 2), (h + 2) % 16,
                          "bA%d_%d" % (h + 2, H), chain=chain, defer=pend_b)
            if pend_b and kt == min(nkt - 1, 5):
                pend_b.pop(0)()
            yield
        sums_av(nkt - 2)
        sums_av(nkt - 1)
        ydst = yflat[:, h * 1024 + qcl * 512: h * 1024 + (qcl + 1) * 512]
        s1, s2 = softmax_tail_steps(h, qc, chain, 512, ydst, "C%d_%d" % (h, qc))
        s1()
        yield
        s2()

    def phase_c_a(b, H):
        for kv in range(NKV):
            norm_item("k", kv, H)
        norm_item("q", 0, H)
        make_bias(attn_bias_a(0), 0, "bA0_%d" % H)
        norm_item("q", 1, H)
        make_bias(attn_bias_a(1), 1, "bA1_%d" % H)

        def chain_iter(c):
            for h in range(c, NH, 2):
                for qcl in range(2):
                    yield from chunk_gen_a(h, qcl, H, c)
        its = [chain_iter(0), chain_iter(1)]
        alive = [True, True]
        while any(alive):
            for c in (0, 1):
                if alive[c]:
                    try:
                        next(its[c])
                    except StopIteration:
                        alive[c] = False

    def chunk_gen_b(h, qcl, H, chain):
        kv = h // (NH // NKV)
        bslot = h % 16
        qc = 4 * H + qcl
        c0 = qc * 256
        av, sr = r_psB[2 * chain], r_psB[2 * chain + 1]
        avc = chain * 1024
        src_ = chain * 1024 + 512
        kt0 = max(0, c0 // 128 - 4)
        kts = list(range(kt0, (c0 + 256) // 128))
        npc = len(kts)
        base = chain * 8
        pend_b = []

        def sums_av(i):
            s = base + (i % 8)
            P.op("tensor", lambda t_, i=i, s=s: t_.matmul(
                psB[0:1, src_:src_ + 256], ones_col[:, :],
                ptv[:, s * 512:s * 512 + 256],
                start=(i == 0), stop=(i == npc - 1)),
                reads=[r_ptg[s // 2], r_consts], writes=[sr],
                tag="Esm%d_%d_%d" % (h, qc, i))
            kt = kts[i]
            P.op("tensor", lambda t_, i=i, s=s, kt=kt: t_.matmul(
                psB[:, avc:avc + 256], vsb[:, kt, kv * 128:(kv + 1) * 128],
                ptv[:, s * 512:s * 512 + 256],
                start=(i == 0), stop=(i == npc - 1)),
                reads=[r_ptg[s // 2], r_vsb], writes=[av],
                tag="Eav%d_%d_%d" % (h, qc, i))

        for i, kt in enumerate(kts):
            pcol = chain * 1024 + (i % 2) * 512
            pg = r_psA[chain]
            s = base + (i % 8)
            P.op("tensor", lambda t_, kt=kt, pcol=pcol: t_.matmul(
                psA[:, pcol:pcol + 256],
                khat[:, kv, kt * 128:(kt + 1) * 128],
                qhat[:, h, qcl * 256:(qcl + 1) * 256],
                start=True, stop=True),
                reads=[r_khat, r_qhat], writes=[pg],
                tag="ES%d_%d_%d" % (h, qc, kt))
            P.op("scalar", lambda s_, pcol=pcol, s=s: s_.activation(
                ptv[:, s * 512:s * 512 + 256], psA[:, pcol:pcol + 256],
                AF.Exp, bias=bias_sb[:, bslot:bslot + 1]),
                reads=[pg, r_biasv[bslot]], writes=[r_ptg[s // 2]],
                tag="Ee%d_%d_%d" % (h, qc, i))
            j = kt - (c0 // 128 - 4)
            mi = {0: 0, 1: 1, 4: 2, 5: 3}.get(j)
            if mi is not None:
                P.op("vector", lambda v, s=s, mi=mi: v.tensor_mul(
                    ptv[:, s * 512:s * 512 + 256],
                    ptv[:, s * 512:s * 512 + 256], mask_b[:, mi, :]),
                    reads=[r_ptg[s // 2], r_consts], writes=[r_ptg[s // 2]],
                    tag="Em%d_%d_%d" % (h, qc, i))
            if i >= 2:
                sums_av(i - 2)
            if i == 2 and qcl == 2 and h + 2 < NH:
                make_bias(attn_bias_b(h + 2), (h + 2) % 16,
                          "bB%d_%d" % (h + 2, H), chain=chain, defer=pend_b)
            if pend_b and i == npc - 1:
                pend_b.pop(0)()
            yield
        for i in range(max(0, npc - 2), npc):
            sums_av(i)
        ydst = yflat[:, h * 1024 + qcl * 256: h * 1024 + (qcl + 1) * 256]
        s1, s2 = softmax_tail_steps(h, qc, chain, 256, ydst, "E%d_%d" % (h, qc))
        s1()
        yield
        s2()

    def phase_c_b(b, H):
        make_bias(attn_bias_b(0), 0, "bB0_%d" % H)
        make_bias(attn_bias_b(1), 1, "bB1_%d" % H)

        def chain_iter(c):
            for h in range(c, NH, 2):
                for qcl in range(4):
                    yield from chunk_gen_b(h, qcl, H, c)
        its = [chain_iter(0), chain_iter(1)]
        alive = [True, True]
        while any(alive):
            for c in (0, 1):
                if alive[c]:
                    try:
                        next(its[c])
                    except StopIteration:
                        alive[c] = False

    def phase_d_attn(b, s, kind, H):
        xsrc = xin[b] if s == first_sub else xres[b]
        xdst = xout[b] if s == last_sub else xres[b]
        slab_res = [[r_qhat], [r_qhat], [r_pt],
                    [r_wq[0], r_wq[1], r_wk[0], r_wk[1]]]
        ld_wo_quarter(kind, 1, 2)
        ld_wo_quarter(kind, 2, 0)
        ld_wo_quarter(kind, 3, 1)
        for pair in range(2):
            bufa, bufb = (3, 2) if pair == 0 else (0, 1)
            for tl in range(8):
                t = H * 8 + tl
                sl = tl % 2
                pg = r_psA[sl]
                P.dma("XD", sl, lambda sy, t=t, sl=sl, pair=pair, xsrc=xsrc: sy.dma_start(
                    xd_buf[sl], xsrc[t * 128:(t + 1) * 128, pair * 1024:(pair + 1) * 1024]),
                    reads=[r_hbm[b][t]], writes=[r_xd[sl]], tag="Dxd%d_%d" % (t, pair))
                for k in range(KT):
                    for ii, bf in enumerate((bufa, bufb)):
                        P.op("tensor", lambda t_, k=k, ii=ii, bf=bf, sl=sl, tl=tl: t_.matmul(
                            psA[:, sl * 1024 + ii * 512: sl * 1024 + (ii + 1) * 512],
                            yT[:, k, tl * 128:(tl + 1) * 128], wo_slab[bf][:, k, :],
                            start=(k == 0), stop=(k == KT - 1)),
                            reads=[r_xlnT] + slab_res[bf], writes=[pg],
                            tag="Do%d_%d_%d_%d" % (t, pair, k, ii))
                P.op("vector", lambda v, sl=sl: v.scalar_tensor_tensor(
                    xn_buf[sl], psAv[sl], 1.0, xd_buf[sl], ALU.mult, ALU.add),
                    reads=[pg, r_xd[sl]], writes=[r_xn[sl]], tag="Dev%d_%d" % (t, pair))
                P.dma("ST", sl, lambda sy, t=t, sl=sl, pair=pair, xdst=xdst: sy.dma_start(
                    xdst[t * 128:(t + 1) * 128, pair * 1024:(pair + 1) * 1024],
                    xn_buf[sl]),
                    reads=[r_xn[sl]], writes=[r_hbm[b][t]], tag="Dst%d_%d" % (t, pair))
                if s != last_sub:
                    cc = t * 16
                    P.op("vector", lambda v, sl=sl, cc=cc, pair=pair: v.tensor_scalar(
                        xln_bf[sl][:, 0:1024], xn_buf[sl],
                        1.0, 0.0, ALU.mult, ALU.add,
                        accum_out=stats8[:, cc + pair:cc + pair + 1]),
                        reads=[r_xn[sl]], writes=[r_xlnb[sl][0], r_st8[t]],
                        tag="Dsu%d_%d" % (t, pair))
                    P.op("scalar", lambda s_, sl=sl, cc=cc, pair=pair: s_.activation(
                        sq_sb[:, :], xn_buf[sl], AF.Square,
                        accum_out=stats8[:, cc + 2 + pair:cc + 3 + pair]),
                        reads=[r_xn[sl]], writes=[r_sq, r_st8[t]],
                        tag="Dsq%d_%d" % (t, pair))
                    if pair == 1:
                        stats_chain2(t)

    def router_tile(tl, pend_tr):
        sl = tl % 2
        lgc = 512 + sl * 512
        lgr = r_psB[1 + sl]
        for k in range(KT):
            P.op("tensor", lambda t_, k=k, tl=tl, lgc=lgc: t_.matmul(
                psB[:, lgc:lgc + NE], xlnT[:, k, tl * 128:(tl + 1) * 128],
                rw_sb[:, k, :], start=(k == 0), stop=(k == KT - 1)),
                reads=[r_xlnT, r_lora], writes=[lgr], tag="Rl%d_%d" % (tl, k))
        P.op("vector", lambda v, lgc=lgc: v.tensor_reduce(
            r_m1s[:, 0:1], psB[:, lgc:lgc + NE], AX.X, ALU.max),
            reads=[lgr], writes=[r_rt], tag="Rm1_%d" % tl)
        P.op("vector", lambda v, lgc=lgc: v.tensor_scalar(
            r_k1s[:, :], psB[:, lgc:lgc + NE], r_m1s[:, 0:1], None, ALU.is_ge),
            reads=[lgr, r_rt], writes=[r_rt], tag="Rk1_%d" % tl)
        P.op("vector", lambda v, lgc=lgc: v.scalar_tensor_tensor(
            r_l2s[:, :], r_k1s[:, :], -1e30, psB[:, lgc:lgc + NE],
            ALU.mult, ALU.add),
            reads=[lgr, r_rt], writes=[r_rt], tag="Rl2_%d" % tl)
        P.op("vector", lambda v: v.tensor_reduce(
            r_m2s[:, 0:1], r_l2s[:, :], AX.X, ALU.max),
            reads=[r_rt], writes=[r_rt], tag="Rm2_%d" % tl)
        P.op("vector", lambda v: v.tensor_scalar(
            r_k2s[:, :], r_l2s[:, :], r_m2s[:, 0:1], None, ALU.is_ge),
            reads=[r_rt], writes=[r_rt], tag="Rk2_%d" % tl)
        P.op("scalar", lambda s_: s_.activation(
            r_e2s[:, 0:1], r_m1s[:, 0:1], AF.Exp, bias=r_m2s[:, 0:1], scale=-1.0),
            reads=[r_rt], writes=[r_rt], tag="Re2_%d" % tl)
        P.op("vector", lambda v: v.tensor_scalar(
            r_s2s[:, 0:1], r_e2s[:, 0:1], 1.0, None, ALU.add),
            reads=[r_rt], writes=[r_rt], tag="Rs2_%d" % tl)
        P.op("vector", lambda v: v.reciprocal(r_w1s[:, 0:1], r_s2s[:, 0:1]),
             reads=[r_rt], writes=[r_rt], tag="Rw1_%d" % tl)
        P.op("vector", lambda v: v.tensor_mul(r_w2s[:, 0:1], r_e2s[:, 0:1], r_w1s[:, 0:1]),
             reads=[r_rt], writes=[r_rt], tag="Rw2_%d" % tl)
        P.op("vector", lambda v: v.tensor_scalar(
            r_t8[:, :], r_k1s[:, :], r_w1s[:, 0:1], None, ALU.mult),
            reads=[r_rt], writes=[r_rt], tag="Rt8_%d" % tl)
        P.op("vector", lambda v: v.scalar_tensor_tensor(
            r_wds[:, :], r_k2s[:, :], r_w2s[:, 0:1], r_t8[:, :], ALU.mult, ALU.add),
            reads=[r_rt], writes=[r_rt], tag="Rwd_%d" % tl)
        P.op("vector", lambda v, tl=tl: v.tensor_copy(
            wds8[:, tl * 8:(tl + 1) * 8], r_wds[:, :]),
            reads=[r_rt], writes=[r_wds8[tl]], tag="Rwm_%d" % tl)

        def _tr(tl=tl):
            P.op("tensor", lambda t_, tl=tl: t_.transpose(
                psB[:, 1536:2048].bitcast(BF16)[0:8, tl * 128:(tl + 1) * 128],
                wds8[:, tl * 8:(tl + 1) * 8], ident[:, :]),
                reads=[r_wds8[tl], r_consts], writes=[r_psB[3]], tag="Rtr_%d" % tl)
        pend_tr.append(_tr)
        if len(pend_tr) > 2:
            pend_tr.pop(0)()

    def phase_r_tail(b, H, pend_tr):
        while pend_tr:
            pend_tr.pop(0)()
        P.op("vector", lambda v: v.tensor_copy(
            wdt[0:8, :], psB[:, 1536:2048].bitcast(BF16)[0:8, 0:1024]),
            reads=[r_psB[3]], writes=[r_wdt, r_ropeo], tag="Rwcp")
        for k in range(KT):
            for c in range(2):
                P.op("tensor", lambda t_, k=k, c=c: t_.matmul(
                    psA[:, c * 512:(c + 1) * 512], la_sb[:, k, :],
                    xlnT[:, k, c * 512:(c + 1) * 512],
                    start=(k == 0), stop=(k == KT - 1)),
                    reads=[r_xlnT, r_lora], writes=[r_psA[0]], tag="Rax%d_%d" % (k, c))
        P.op("scalar", lambda s_: s_.copy(axt[:, :], psA[:, 0:1024]),
             reads=[r_psA[0]], writes=[r_axt], tag="Raxc")
        for c in range(2):
            P.op("tensor", lambda t_, c=c: t_.matmul(
                psB[:, c * 512:(c + 1) * 512], rep[:, :], wdt[0:8, c * 512:(c + 1) * 512],
                start=True, stop=True),
                reads=[r_wdt, r_ropeo, r_consts], writes=[r_psB[c]], tag="Rrep%d" % c)
        P.op("vector", lambda v: v.tensor_mul(gt_sb[:, :], axt[:, :], psB[:, 0:1024]),
             reads=[r_axt, r_psB[0], r_psB[1]], writes=[r_gt], tag="Rgt")

    def phase_d_ffn(b, s, H):
        xsrc = xin[b] if s == first_sub else xres[b]
        xdst = xout[b] if s == last_sub else xres[b]
        for tl in range(8):
            t = H * 8 + tl
            sl = tl % 2
            ps = psA if sl == 0 else psB
            prs = [r_psA[0], r_psA[1]] if sl == 0 else [r_psB[0], r_psB[1], r_psB[2], r_psB[3]]
            xor = r_xf[sl]
            xnr = [r_xd[0], r_xd[1]] if sl == 0 else [r_xn[0], r_xn[1]]
            P.dma("XD", sl, lambda sy, t=t, sl=sl, xsrc=xsrc: sy.dma_start(
                xf_buf[sl], xsrc[t * 128:(t + 1) * 128, :]),
                reads=[r_hbm[b][t]], writes=xor, tag="Fxd%d" % t)
            for c in range(4):
                for k in range(KT):
                    P.op("tensor", lambda t_, k=k, c=c, ps=ps, tl=tl: t_.matmul(
                        ps[:, c * 512:(c + 1) * 512],
                        xlnT[:, k, tl * 128:(tl + 1) * 128], wffn[:, k, c * 512:(c + 1) * 512],
                        start=(k == 0), stop=False),
                        reads=[r_xlnT, r_wffn], writes=prs, tag="Fb%d_%d_%d" % (t, c, k))
                P.op("tensor", lambda t_, c=c, ps=ps, tl=tl: t_.matmul(
                    ps[:, c * 512:(c + 1) * 512],
                    gt_sb[:, tl * 128:(tl + 1) * 128], lb_sb[:, c * 512:(c + 1) * 512],
                    start=False, stop=True),
                    reads=[r_gt, r_lora], writes=prs, tag="Fl%d_%d" % (t, c))
            P.op("vector", lambda v, sl=sl, ps=ps: v.scalar_tensor_tensor(
                xnf_buf[sl], ps[:, 0:2048], 1.0, xf_buf[sl], ALU.mult, ALU.add),
                reads=prs + xor, writes=xnr, tag="Fev%d" % t)
            P.dma("ST", sl, lambda sy, t=t, sl=sl, xdst=xdst: sy.dma_start(
                xdst[t * 128:(t + 1) * 128, :], xnf_buf[sl]),
                reads=xnr, writes=[r_hbm[b][t]], tag="Fst%d" % t)
            if s != last_sub:
                cc = t * 16
                P.op("vector", lambda v, sl=sl, cc=cc: v.tensor_scalar(
                    xln_bf[sl][:, :], xnf_buf[sl],
                    1.0, 0.0, ALU.mult, ALU.add,
                    accum_out=stats8[:, cc:cc + 1]),
                    reads=xnr, writes=r_xlnb[sl] + [r_st8[t]], tag="Fsu%d" % t)
                P.op("scalar", lambda s_, sl=sl, cc=cc: s_.activation(
                    psA[:, 0:2048] if sl == 1 else psB[:, 0:2048],
                    xnf_buf[sl], AF.Square,
                    accum_out=stats8[:, cc + 2:cc + 3]),
                    reads=xnr,
                    writes=([r_psA[0], r_psA[1]] if sl == 1 else
                            [r_psB[0], r_psB[1], r_psB[2], r_psB[3]]) + [r_st8[t]],
                    tag="Fsq%d" % t)
                stats_chain1(t)

    for b in range(nb):
        for s in sublayers:
            kind = SUBLAYERS[s]
            if kind == "ffn":
                ld_ffn_weights()
                for H in range(2):
                    pend_tr = []
                    if s != first_sub:
                        phase_a(b, s, H,
                                tile_cb=lambda tl, p=pend_tr: router_tile(tl, p))
                    else:
                        phase_a(b, s, H)
                        for tl in range(8):
                            router_tile(tl, pend_tr)
                    phase_r_tail(b, H, pend_tr)
                    phase_d_ffn(b, s, H)
            else:
                arena_handoff([r_wffn, r_lora, r_gt, r_axt], ALL_BIG_ATTN)
                for H in range(2):
                    vk = kind if s != first_sub else None
                    phase_a(b, s, H, vkind=vk)
                    phase_b(b, kind, H, do_v=(vk is None))
                    ld_wo_quarter(kind, 0, 3)
                    if kind == "attn_a":
                        phase_c_a(b, H)
                    else:
                        phase_c_b(b, H)
                    phase_d_attn(b, s, kind, H)

    P.op("sync", lambda sy: sy.nop(), deps=list(P.last_store.values()), tag="final")
    P.check()

    sem_stack = ExitStack()
    sems = {}
    for name in P.sem_names():
        sems[name] = sem_stack.enter_context(nc.semaphore("s" + name))
    with nc.Block() as block:
        @block.tensor
        def _(t_):
            P.emit_engine(t_, "tensor", sems)

        @block.vector
        def _(v):
            P.emit_engine(v, "vector", sems)

        @block.scalar
        def _(s_):
            P.emit_engine(s_, "scalar", sems)

        @block.gpsimd
        def _(g):
            P.emit_engine(g, "gpsimd", sems)

        @block.sync
        def _(sy):
            P.emit_engine(sy, "sync", sems)

    sem_stack.close()
    st.close()
    nc._dev_aps = {"xout": xout, "xin": xin, "xres": xres}
    nc._dev_prog = P
    return nc


def kernel(**inputs):
    A = host_arrays(inputs)
    x = np.ascontiguousarray(inputs["L0_attn_a_x"], dtype=np.float32)
    nc = build_nc()
    in_maps = []
    for c in range(NCORES):
        m = dict(A)
        m["xin"] = x[c * NB:(c + 1) * NB]
        in_maps.append(m)
    res = run_bass_kernel_spmd(nc, in_maps, core_ids=list(range(NCORES)))
    out = np.concatenate([r["xout"] for r in res.results], axis=0)
    return out.astype(np.float32)


# version 26
# speedup vs baseline: 1.0201x; 1.0201x over previous
import math
from contextlib import ExitStack

import numpy as np
import concourse.bass as bass
import concourse.mybir as mybir
from concourse.bass_utils import run_bass_kernel_spmd

F32 = mybir.dt.float32
BF16 = mybir.dt.bfloat16
AF = mybir.ActivationFunctionType
ALU = mybir.AluOpType
AX = mybir.AxisListType

S = 2048
DM = 2048
HD = 128
NH = 16
NKV = 4
WIN = 512
NB = 2
NCORES = 8
KT = DM // 128
NT = S // 128
EPS_LN = 1e-5
EPS_RMS = 1.1920928955078125e-07
THETA = 10000.0
NE = 8
RANK = 16
SCALING = 32.0 / RANK
ISD = 1.0 / math.sqrt(HD)

ENGS = ("tensor", "vector", "scalar", "gpsimd", "sync")


class _Op:
    __slots__ = ("eng", "emit", "deps", "val", "is_dma", "sem", "tag")

    def __init__(self, eng, emit, is_dma, tag):
        self.eng = eng
        self.emit = emit
        self.deps = []
        self.val = None
        self.sem = None
        self.is_dma = is_dma
        self.tag = tag


class Res:
    __slots__ = ("name", "last_w", "readers")

    def __init__(self, name):
        self.name = name
        self.last_w = None
        self.readers = []


class Prog:
    def __init__(self):
        self.seq = {e: [] for e in ENGS}
        self.eng_ctr = {e: 0 for e in ENGS}
        self.dma_sems = {}
        self.dma_eng = {}
        self.last_store = {}

    def channel(self, name, eng):
        self.dma_eng[name] = eng

    def op(self, eng, emit, reads=(), writes=(), tag="", deps=()):
        o = _Op(eng, emit, False, tag)
        self.eng_ctr[eng] += 1
        o.val = self.eng_ctr[eng]
        o.sem = "E_" + eng
        self._wire(o, reads, writes, deps)
        self.seq[eng].append(o)
        return o

    def dma(self, ch, slot, emit, reads=(), writes=(), tag="", deps=()):
        key = (ch, slot)
        if key not in self.dma_sems:
            self.dma_sems[key] = ["D_%s_%s" % (ch, slot), 0, None]
        ent = self.dma_sems[key]
        ent[1] += 1
        o = _Op(self.dma_eng[ch], emit, True, tag)
        o.sem = ent[0]
        o.val = ent[1] * 16
        deps = list(deps) + ([ent[2]] if ent[2] is not None else [])
        ent[2] = o
        self._wire(o, reads, writes, deps)
        self.seq[o.eng].append(o)
        if ch == "ST":
            self.last_store[o.sem] = o
        return o

    def _wire(self, o, reads, writes, deps):
        dd = {}
        for d in deps:
            if d is not None:
                dd[id(d)] = d
        for r in reads:
            if r.last_w is not None:
                dd[id(r.last_w)] = r.last_w
            r.readers.append(o)
        for w in writes:
            if w.last_w is not None:
                dd[id(w.last_w)] = w.last_w
            for d in w.readers:
                dd[id(d)] = d
            w.last_w = o
            w.readers = []
        o.deps = [d for d in dd.values() if d is not o]

    def sem_names(self):
        return ["E_" + e for e in ENGS] + [v[0] for v in self.dma_sems.values()]

    def emit_engine(self, engine_obj, eng, sems):
        watermark = {}
        for o in self.seq[eng]:
            needs = {}
            for d in o.deps:
                if eng == "tensor" and d.eng == "tensor" and not d.is_dma:
                    continue
                if needs.get(d.sem, -1) < d.val:
                    needs[d.sem] = d.val
            for k in sorted(needs):
                v = needs[k]
                if watermark.get(k, -1) >= v:
                    continue
                engine_obj.wait_ge(sems[k], v)
                watermark[k] = v
            inst = o.emit(engine_obj)
            if inst is None:
                raise RuntimeError("emit None: %s" % o.tag)
            inst.then_inc(sems[o.sem], 16 if o.is_dma else 1)

    def check(self):
        idx = {e: 0 for e in ENGS}
        semv = {}
        total = sum(len(self.seq[e]) for e in ENGS)
        done = 0
        progress = True
        while progress:
            progress = False
            for e in ENGS:
                while idx[e] < len(self.seq[e]):
                    o = self.seq[e][idx[e]]
                    if not all(semv.get(d.sem, 0) >= d.val for d in o.deps):
                        break
                    semv[o.sem] = max(semv.get(o.sem, 0), o.val)
                    idx[e] += 1
                    done += 1
                    progress = True
        if done != total:
            for e in ENGS:
                if idx[e] < len(self.seq[e]):
                    o = self.seq[e][idx[e]]
                    missing = [(d.sem, d.val, semv.get(d.sem, 0), d.tag)
                               for d in o.deps if semv.get(d.sem, 0) < d.val]
                    raise RuntimeError("deadlock %s[%d] %s missing=%s"
                                       % (e, idx[e], o.tag, missing[:4]))


def _rope_tables():
    inv = THETA ** (-np.arange(64, dtype=np.float64) / 64.0)
    f = np.arange(S, dtype=np.float64)[None, :] * inv[:, None]
    c = np.cos(f)
    s = np.sin(f)
    tabc = np.concatenate([c, c], 0).astype(np.float32)
    tabs = np.concatenate([-s, s], 0).astype(np.float32)
    return tabc, tabs


def _masks_a():
    k = np.arange(128)[:, None]
    q = np.arange(512)[None, :]
    return np.stack([(q >= 128 * j + k) for j in range(4)]).astype(np.float32)


def _masks_b():
    k = np.arange(128)[:, None]
    q = np.arange(256)[None, :]
    out = []
    for j in (0, 1, 4, 5):
        m = ((q - k) <= (128 * j - 1)) & ((q - k) >= (128 * j - 512))
        out.append(m.astype(np.float32))
    return np.stack(out)


def _tile_w(w):
    K, M = w.shape
    return np.ascontiguousarray(w.reshape(K // 128, 128, M))


def _head_slab(w):
    K, M = w.shape
    return np.ascontiguousarray(
        w.reshape(K // 128, 128, M // 128, 128).transpose(2, 0, 1, 3))


def _quarter_slab(w):
    K, M = w.shape
    return np.ascontiguousarray(
        w.reshape(K // 128, 128, M // 512, 512).transpose(2, 0, 1, 3))


def host_arrays(inputs):
    A = {}
    A["wq_a"] = _head_slab(inputs["L0_attn_a_Wq"])
    A["wk_a"] = _head_slab(inputs["L0_attn_a_Wk"])
    A["wv_a"] = _tile_w(inputs["L0_attn_a_Wv"])
    A["wo_a"] = _quarter_slab(inputs["L0_attn_a_Wo"])
    A["wq_b"] = _head_slab(np.ascontiguousarray(inputs["L0_attn_b_Wq"].T))
    A["wk_b"] = _head_slab(np.ascontiguousarray(inputs["L0_attn_b_Wk"].T))
    A["wv_b"] = _tile_w(np.ascontiguousarray(inputs["L0_attn_b_Wv"].T))
    A["wo_b"] = _quarter_slab(np.ascontiguousarray(inputs["L0_attn_b_Wo"].T))
    A["gain"] = np.ascontiguousarray(
        inputs["L0_attn_b_q_gain"].reshape(1, NH).astype(np.float32))
    A["w_ffn"] = _tile_w(np.ascontiguousarray(inputs["L0_ffn_weight"].T))
    la = inputs["L0_ffn_lora_A"].reshape(NE * RANK, DM)
    A["la"] = _tile_w(np.ascontiguousarray(la.T))
    lb = inputs["L0_ffn_lora_B"].transpose(0, 2, 1).reshape(NE * RANK, DM)
    A["lb"] = np.ascontiguousarray(lb)
    A["rw"] = _tile_w(np.ascontiguousarray(inputs["L0_ffn_router_w"].T))
    A["ident"] = np.eye(128, dtype=np.float32)
    A["swp"] = np.roll(np.eye(128, dtype=np.float32), 64, axis=1)
    A["tabc"], A["tabs"] = _rope_tables()
    A["mask_a"] = _masks_a()
    A["mask_b"] = _masks_b()
    A["rep"] = np.ascontiguousarray(
        np.repeat(np.eye(NE, dtype=np.float32), RANK, axis=1) * SCALING)
    fc = np.array([EPS_LN, EPS_RMS, math.log(ISD), 0.0], dtype=np.float32)
    A["fconst"] = np.ascontiguousarray(np.broadcast_to(fc, (128, 4)).copy())
    return A


SUBLAYERS = ("attn_a", "ffn", "attn_b", "ffn")


def build_nc(sublayers=(0, 1, 2, 3), nb=NB):
    nc = bass.Bass("TRN2", target_bir_lowering=False)
    P = Prog()
    P.channel("W", "gpsimd")
    P.channel("XA", "sync")
    P.channel("XD", "sync")
    P.channel("ST", "gpsimd")

    last_sub = sublayers[-1]
    first_sub = sublayers[0]

    xin = nc.dram_tensor("xin", [nb, S, DM], F32, kind="ExternalInput").ap()
    xout = nc.dram_tensor("xout", [nb, S, DM], F32, kind="ExternalOutput").ap()
    xres = nc.dram_tensor("xres", [nb, S, DM], F32, kind="Internal").ap()
    dr = {}
    for name, shape in [
        ("wq_a", [NH, KT, 128, 128]), ("wk_a", [NKV, KT, 128, 128]),
        ("wv_a", [KT, 128, 512]), ("wo_a", [4, KT, 128, 512]),
        ("wq_b", [NH, KT, 128, 128]), ("wk_b", [NKV, KT, 128, 128]),
        ("wv_b", [KT, 128, 512]), ("wo_b", [4, KT, 128, 512]),
        ("gain", [1, NH]), ("w_ffn", [KT, 128, DM]), ("la", [KT, 128, 128]),
        ("lb", [128, DM]), ("rw", [KT, 128, NE]),
        ("ident", [128, 128]), ("swp", [128, 128]),
        ("tabc", [128, S]), ("tabs", [128, S]),
        ("mask_a", [4, 128, 512]), ("mask_b", [4, 128, 256]), ("rep", [NE, 128]),
        ("fconst", [128, 4]),
    ]:
        dr[name] = nc.dram_tensor(name, shape, F32, kind="ExternalInput").ap()

    st = ExitStack()
    E = st.enter_context

    ident = E(nc.sbuf_tensor("c_ident", [128, 128], BF16))
    ones_col = E(nc.sbuf_tensor("c_ones_col", [128, 1], BF16))
    ones_row = E(nc.sbuf_tensor("c_ones_row", [1, 128], F32))
    swp_c = E(nc.sbuf_tensor("c_swp", [128, 128], BF16))
    tabc = E(nc.sbuf_tensor("c_tabc", [128, S], BF16))
    tabs = E(nc.sbuf_tensor("c_tabs", [128, S], BF16))
    mask_a = E(nc.sbuf_tensor("c_mask_a", [128, 4, 512], BF16))
    mask_b = E(nc.sbuf_tensor("c_mask_b", [128, 4, 256], BF16))
    rep = E(nc.sbuf_tensor("c_rep", [NE, 128], BF16))
    gain = E(nc.sbuf_tensor("c_gain", [1, NH], F32))
    fc_sb = E(nc.sbuf_tensor("c_fconst", [128, 4], F32))

    big = E(nc.sbuf_tensor("big", [128, 40960], BF16))
    khat = big[:, 0:8192].rearrange("p (a c) -> p a c", a=NKV)
    vsb = big[:, 8192:16384].rearrange("p (a c) -> p a c", a=NT)
    qhat = big[:, 16384:32768].rearrange("p (a c) -> p a c", a=NH)
    ptv = big[:, 32768:40960]
    pt = ptv.rearrange("p (a c) -> p a c", a=16)
    wffn = big[:, 0:32768].rearrange("p (a c) -> p a c", a=KT)
    la_sb = big[:, 32768:34816].rearrange("p (a c) -> p a c", a=KT)
    lb_sb = big[:, 34816:36864]
    rw_sb = big[:, 36864:36992].rearrange("p (a c) -> p a c", a=KT)
    axt = big[:, 36992:38016]
    gt_sb = big[:, 38016:39040]
    wv_full = big[:, 32768:40960].rearrange("p (k c) -> p k c", k=KT)
    wo_slab = [big[:, 16384:24576].rearrange("p (k c) -> p k c", k=KT),
               big[:, 24576:32768].rearrange("p (k c) -> p k c", k=KT),
               big[:, 32768:40960].rearrange("p (k c) -> p k c", k=KT),
               None]

    xlnT = E(nc.sbuf_tensor("xlnT", [128, KT, 1024], BF16))
    yT = xlnT
    yflat = xlnT[:, :, :].rearrange("p a c -> p (a c)")

    wsl_ar = E(nc.sbuf_tensor("wsl_ar", [128, 8192], BF16))
    wq_slab = wsl_ar[:, 0:4096].rearrange("p (s k c) -> p s k c", s=2, k=KT)
    wk_slab = wsl_ar[:, 4096:8192].rearrange("p (s k c) -> p s k c", s=2, k=KT)
    wo_sl0 = wsl_ar[:, 0:8192].rearrange("p (k c) -> p k c", k=KT)
    wv_slab = E(nc.sbuf_tensor("wv_slab", [128, 2, 512], BF16))
    wo_slab[3] = wo_sl0

    xar = E(nc.sbuf_tensor("xar", [128, 8192], F32))
    xf_buf = [xar[:, 0:2048], xar[:, 2048:4096]]
    xd_buf = [xar[:, 4096:5120], xar[:, 5120:6144]]
    xn_buf = [xar[:, 6144:7168], xar[:, 7168:8192]]
    xnf_buf = [xar[:, 4096:6144], xar[:, 6144:8192]]

    xln_bf = [E(nc.sbuf_tensor("xlnb%d" % i, [128, DM], BF16)) for i in range(2)]
    sq_sb = E(nc.sbuf_tensor("sq_sb", [128, 1024], BF16))
    _rawbf = E(nc.sbuf_tensor("rawbf", [128, 1024], BF16))
    raw_bf = [_rawbf, _rawbf]
    swp_bf = E(nc.sbuf_tensor("swpbf", [128, 1024], BF16))
    rope_t2 = E(nc.sbuf_tensor("rope_t2", [128, 1024], BF16))
    rope_tmp = E(nc.sbuf_tensor("rope_tmp", [128, 1024], BF16))
    rope_o = E(nc.sbuf_tensor("rope_o", [128, 1024], BF16))
    stats = [E(nc.sbuf_tensor("stats%d" % i, [128, 8], F32)) for i in range(2)]
    stats8 = E(nc.sbuf_tensor("stats8", [128, 256], F32))
    rstd = [E(nc.sbuf_tensor("rstd%d" % i, [128, 1], F32)) for i in range(2)]
    _trow = E(nc.sbuf_tensor("trow", [1, 1024], F32))
    trow = [_trow, _trow]
    _rc = E(nc.sbuf_tensor("rc", [1, 512], F32))
    _rbc = E(nc.sbuf_tensor("rbc", [128, 512], F32))
    rc_sb = [_rc, _trow[0:1, 0:512]]
    rbc = [_rbc, rope_tmp[:, :].bitcast(F32)[:, 0:512]]
    bias_sb = E(nc.sbuf_tensor("bias_sb", [128, 16], F32))
    bnd = E(nc.sbuf_tensor("bnd", [1, 8], F32))
    qn2 = E(nc.sbuf_tensor("qn2", [1, NH], F32))
    kn2 = E(nc.sbuf_tensor("kn2", [1, NKV], F32))
    wdt = rope_o
    r_m1s = E(nc.sbuf_tensor("r_m1s", [128, 1], F32))
    r_m2s = E(nc.sbuf_tensor("r_m2s", [128, 1], F32))
    r_l2s = E(nc.sbuf_tensor("r_l2s", [128, NE], F32))
    r_k1s = E(nc.sbuf_tensor("r_k1s", [128, NE], F32))
    r_k2s = E(nc.sbuf_tensor("r_k2s", [128, NE], F32))
    r_e2s = E(nc.sbuf_tensor("r_e2s", [128, 1], F32))
    r_s2s = E(nc.sbuf_tensor("r_s2s", [128, 1], F32))
    r_w1s = E(nc.sbuf_tensor("r_w1s", [128, 1], F32))
    r_w2s = E(nc.sbuf_tensor("r_w2s", [128, 1], F32))
    r_wds = E(nc.sbuf_tensor("r_wds", [128, NE], BF16))
    r_t8 = E(nc.sbuf_tensor("r_t8", [128, NE], F32))
    wds8 = E(nc.sbuf_tensor("wds8", [128, 64], BF16))

    psA = E(nc.psum_tensor("psA", [128, 2048], F32))
    psB = E(nc.psum_tensor("psB", [128, 2048], F32))
    psAv = [psA[:, 0:1024], psA[:, 1024:2048]]
    psBs = [psB[:, 0:512], psB[:, 512:1024], psB[:, 1024:1536], psB[:, 1536:2048]]
    tr_banks = [psA[:, i * 512:(i + 1) * 512].bitcast(BF16) for i in range(4)] + \
               [psB[:, i * 512:(i + 1) * 512].bitcast(BF16) for i in range(4)]

    r_xf = [[Res("xf0a"), Res("xf0b")], [Res("xf1a"), Res("xf1b")]]
    r_xd = [Res("xd0"), Res("xd1")]
    r_xn = [Res("xn0"), Res("xn1")]
    r_xlnb = [[Res("xlnb0a"), Res("xlnb0b")], [Res("xlnb1a"), Res("xlnb1b")]]
    r_stats = [Res("st0"), Res("st1")]
    r_st8 = [Res("st8_%d" % i) for i in range(NT)]
    r_rstd = [Res("rs0"), Res("rs1")]
    r_xlnT = Res("xlnT")
    r_qhat = Res("qhat")
    r_pt = Res("pt")
    r_ptg = [Res("ptg%d" % i) for i in range(8)]
    r_khat = Res("khat")
    r_vsb = Res("vsb")
    r_wq = [Res("wq0"), Res("wq1")]
    r_wk = [Res("wk0"), Res("wk1")]
    r_wv = [Res("wv0"), Res("wv1")]
    r_wffn = Res("wffn")
    r_lora = Res("lora")
    r_consts = Res("consts")
    r_sq = Res("sq")
    _rraw = Res("raw")
    r_raw = [_rraw, _rraw]
    r_swpb = Res("swpb")
    r_ropet = Res("ropet")
    r_ropet2 = Res("ropet2")
    r_ropeo = Res("ropeo")
    _rtrow = Res("trow")
    r_trow = [_rtrow, _rtrow]
    r_rc = [Res("rc"), _rtrow]
    r_rbc = [Res("rbc"), r_ropet]
    r_bias = Res("bias")
    r_biasv = [Res("bias%d" % i) for i in range(16)]
    r_bnd = Res("bnd")
    r_qn = Res("qn")
    r_kn = Res("kn")
    r_rt = Res("router")
    r_gt = Res("gt")
    r_axt = Res("axt")
    r_wdt = Res("wdt")
    r_wds8 = [Res("wds8_%d" % i) for i in range(8)]
    r_psA = [Res("psA0"), Res("psA1")]
    r_psB = [Res("psB0"), Res("psB1"), Res("psB2"), Res("psB3")]
    r_hbm = [[Res("hbm_%d_%d" % (b, t)) for t in range(NT)] for b in range(nb)]

    r_trb = [r_psA[0], r_psA[0], r_psA[1], r_psA[1],
             r_psB[0], r_psB[1], r_psB[2], r_psB[3]]
    ALL_BIG_ATTN = [r_khat, r_vsb, r_qhat, r_pt]

    def arena_handoff(olds, news):
        pend = []
        for r in olds:
            if r.last_w is not None:
                pend.append(r.last_w)
            pend.extend(r.readers)
        seen = {id(p): p for p in pend}
        for r in news:
            have = {id(x) for x in r.readers}
            for p in seen.values():
                if id(p) not in have:
                    r.readers.append(p)

    P.dma("W", "cons", lambda g: g.dma_start(ident[:, :], dr["ident"]),
          writes=[r_consts], tag="ld_ident")
    P.dma("W", "cons", lambda g: g.dma_start(swp_c[:, :], dr["swp"]),
          writes=[r_consts], tag="ld_swp")
    P.dma("W", "cons", lambda g: g.dma_start(tabc[:, :], dr["tabc"]),
          writes=[r_consts], tag="ld_tabc")
    P.dma("W", "cons", lambda g: g.dma_start(tabs[:, :], dr["tabs"]),
          writes=[r_consts], tag="ld_tabs")
    P.dma("W", "cons", lambda g: g.dma_start(
        mask_a[:, :, :], dr["mask_a"].rearrange("j p q -> p j q")),
        writes=[r_consts], tag="ld_ma")
    P.dma("W", "cons", lambda g: g.dma_start(
        mask_b[:, :, :], dr["mask_b"].rearrange("j p q -> p j q")),
        writes=[r_consts], tag="ld_mb")
    P.dma("W", "cons", lambda g: g.dma_start(rep[:, :], dr["rep"]),
          writes=[r_consts], tag="ld_rep")
    P.dma("W", "cons", lambda g: g.dma_start(gain[:, :], dr["gain"]),
          writes=[r_consts], tag="ld_gain")
    P.dma("W", "cons", lambda g: g.dma_start(fc_sb[:, :], dr["fconst"]),
          writes=[r_consts], tag="ld_fc")
    P.op("vector", lambda v: v.memset(ones_col[:, :], 1.0), writes=[r_consts], tag="ones_c")
    P.op("vector", lambda v: v.memset(ones_row[:, :], 1.0), writes=[r_consts], tag="ones_r")

    def ld_wq_head(kind, h, slot):
        nm = "wq_a" if kind == "attn_a" else "wq_b"
        return P.dma("W", "wq%d" % slot, lambda g, h=h, slot=slot, nm=nm: g.dma_start(
            wq_slab[:, slot, :, :], dr[nm][h].rearrange("k p c -> p k c")),
            writes=[r_wq[slot]], tag="ld_wq%d" % h)

    def ld_wk_head(kind, kv, slot):
        nm = "wk_a" if kind == "attn_a" else "wk_b"
        return P.dma("W", "wk%d" % slot, lambda g, kv=kv, slot=slot, nm=nm: g.dma_start(
            wk_slab[:, slot, :, :], dr[nm][kv].rearrange("k p c -> p k c")),
            writes=[r_wk[slot]], tag="ld_wk%d" % kv)

    def ld_wv_full(kind):
        nm = "wv_a" if kind == "attn_a" else "wv_b"
        return P.dma("W", "wvf", lambda g, nm=nm: g.dma_start(
            wv_full[:, :, :], dr[nm].rearrange("k p c -> p k c")),
            writes=[r_pt] + r_ptg, tag="ld_wvf")

    def ld_wv_tile(kind, k, slot):
        nm = "wv_a" if kind == "attn_a" else "wv_b"
        return P.dma("W", "wv%d" % slot, lambda g, k=k, slot=slot, nm=nm: g.dma_start(
            wv_slab[:, slot, :], dr[nm][k]),
            writes=[r_wv[slot]], tag="ld_wv%d" % k)

    def ld_ffn_weights():
        P.dma("W", "ffw", lambda g: g.dma_start(
            wffn[:, :, :], dr["w_ffn"].rearrange("k p c -> p k c")),
            writes=[r_wffn] + ALL_BIG_ATTN[:3], tag="ld_wffn")
        P.dma("W", "ffl", lambda g: g.dma_start(
            la_sb[:, :, :], dr["la"].rearrange("k p c -> p k c")),
            writes=[r_lora, r_pt] + r_ptg, tag="ld_la")
        P.dma("W", "ffl2", lambda g: g.dma_start(lb_sb[:, :], dr["lb"]),
              writes=[r_lora, r_pt], tag="ld_lb")
        P.dma("W", "ffl3", lambda g: g.dma_start(
            rw_sb[:, :, :], dr["rw"].rearrange("k p c -> p k c")),
            writes=[r_lora, r_pt], tag="ld_rw")

    def ld_wo_quarter(kind, q, buf_idx):
        nm = "wo_a" if kind == "attn_a" else "wo_b"
        rr = [[r_qhat], [r_qhat], [r_pt] + r_ptg,
              [r_wq[0], r_wq[1], r_wk[0], r_wk[1]]][buf_idx]
        return P.dma("W", "wo%d" % buf_idx, lambda g, q=q, buf_idx=buf_idx, nm=nm: g.dma_start(
            wo_slab[buf_idx][:, :, :], dr[nm][q].rearrange("k p c -> p k c")),
            writes=rr, tag="ld_wo%d" % q)


    def _stats_tail(t, c, rr):
        P.op("vector", lambda v: v.tensor_scalar(
            stats8[:, c + 8:c + 9], stats8[:, c + 0:c + 1], 1.0 / DM, None, ALU.mult),
            reads=rr, writes=rr, tag="SCm%d" % t)
        P.op("vector", lambda v: v.tensor_scalar(
            stats8[:, c + 9:c + 10], stats8[:, c + 8:c + 9], -1.0, None, ALU.mult),
            reads=rr, writes=rr, tag="SCn%d" % t)
        P.op("vector", lambda v: v.tensor_scalar(
            stats8[:, c + 11:c + 12], stats8[:, c + 2:c + 3], 1.0 / DM, None, ALU.mult),
            reads=rr, writes=rr, tag="SCe%d" % t)
        P.op("vector", lambda v: v.scalar_tensor_tensor(
            stats8[:, c + 9:c + 10], stats8[:, c + 9:c + 10], stats8[:, c + 8:c + 9],
            stats8[:, c + 11:c + 12], ALU.mult, ALU.add),
            reads=rr, writes=rr, tag="SCv%d" % t)
        P.op("scalar", lambda s_: s_.activation(
            stats8[:, c + 9:c + 10], stats8[:, c + 9:c + 10], AF.Ln, bias=fc_sb[:, 0:1]),
            reads=rr + [r_consts], writes=rr, tag="SCl%d" % t)
        P.op("scalar", lambda s_: s_.activation(
            stats8[:, c + 10:c + 11], stats8[:, c + 9:c + 10], AF.Exp, scale=-0.5),
            reads=rr, writes=rr, tag="SCr%d" % t)

    def stats_chain2(t):
        c = t * 16
        rr = [r_st8[t]]
        P.op("vector", lambda v: v.tensor_add(
            stats8[:, c + 0:c + 1], stats8[:, c + 0:c + 1], stats8[:, c + 1:c + 2]),
            reads=rr, writes=rr, tag="SCa%d" % t)
        P.op("vector", lambda v: v.tensor_add(
            stats8[:, c + 2:c + 3], stats8[:, c + 2:c + 3], stats8[:, c + 3:c + 4]),
            reads=rr, writes=rr, tag="SCd%d" % t)
        _stats_tail(t, c, rr)

    def stats_chain4(t):
        c = t * 16
        rr = [r_st8[t]]
        P.op("vector", lambda v: v.tensor_add(
            stats8[:, c + 0:c + 1], stats8[:, c + 0:c + 1], stats8[:, c + 1:c + 2]),
            reads=rr, writes=rr, tag="SCa%d" % t)
        P.op("vector", lambda v: v.tensor_add(
            stats8[:, c + 4:c + 5], stats8[:, c + 4:c + 5], stats8[:, c + 5:c + 6]),
            reads=rr, writes=rr, tag="SCb%d" % t)
        P.op("vector", lambda v: v.tensor_add(
            stats8[:, c + 0:c + 1], stats8[:, c + 0:c + 1], stats8[:, c + 4:c + 5]),
            reads=rr, writes=rr, tag="SCc%d" % t)
        P.op("vector", lambda v: v.tensor_add(
            stats8[:, c + 2:c + 3], stats8[:, c + 2:c + 3], stats8[:, c + 3:c + 4]),
            reads=rr, writes=rr, tag="SCd%d" % t)
        P.op("vector", lambda v: v.tensor_add(
            stats8[:, c + 6:c + 7], stats8[:, c + 6:c + 7], stats8[:, c + 7:c + 8]),
            reads=rr, writes=rr, tag="SCf%d" % t)
        P.op("vector", lambda v: v.tensor_add(
            stats8[:, c + 2:c + 3], stats8[:, c + 2:c + 3], stats8[:, c + 6:c + 7]),
            reads=rr, writes=rr, tag="SCg%d" % t)
        _stats_tail(t, c, rr)

    def stats_chain1(t):
        _stats_tail(t, t * 16, [r_st8[t]])

    def phase_a(b, s, H, vkind=None, tile_cb=None):
        if s != first_sub:
            return phase_a_apply(b, s, H, vkind=vkind, tile_cb=tile_cb)
        xsrc = xin[b]

        def lda(tl):
            t = H * 8 + tl
            j = tl % 4
            P.dma("XA", j, lambda sy, t=t, j=j: sy.dma_start(
                a_stage[j][:, 0:1024], xsrc[t * 128:(t + 1) * 128, 0:1024]),
                reads=[r_hbm[b][t]], writes=a_stage_hres[j][0], tag="Ald%d" % t)
            P.dma("XD", j, lambda sy, t=t, j=j: sy.dma_start(
                a_stage[j][:, 1024:2048], xsrc[t * 128:(t + 1) * 128, 1024:2048]),
                reads=[r_hbm[b][t]], writes=a_stage_hres[j][1], tag="Ald2%d" % t)

        for tl in range(4):
            lda(tl)
        for tl in range(8):
            t = H * 8 + tl
            sl = tl % 2
            j = tl % 4
            P.op("vector", lambda v, sl=sl, j=j: v.tensor_scalar(
                xln_bf[sl][:, :], a_stage[j], 1.0, 0.0, ALU.mult, ALU.add,
                accum_out=stats[sl][:, 0:1]),
                reads=a_stage_res[j], writes=r_xlnb[sl] + [r_stats[sl]], tag="Asum%d" % t)
            P.op("scalar", lambda s_, sl=sl, j=j: s_.activation(
                psB[:, :], a_stage[j], AF.Square, accum_out=stats[sl][:, 1:2]),
                reads=a_stage_res[j],
                writes=[r_psB[0], r_psB[1], r_psB[2], r_psB[3], r_stats[sl]],
                tag="Asq%d" % t)
            P.op("vector", lambda v, sl=sl: v.tensor_scalar(
                stats[sl][:, 2:3], stats[sl][:, 0:1], 1.0 / DM, None, ALU.mult),
                reads=[r_stats[sl]], writes=[r_stats[sl]], tag="Am%d" % t)
            P.op("vector", lambda v, sl=sl: v.tensor_scalar(
                stats[sl][:, 3:4], stats[sl][:, 1:2], 1.0 / DM, None, ALU.mult),
                reads=[r_stats[sl]], writes=[r_stats[sl]], tag="Ams%d" % t)
            P.op("vector", lambda v, sl=sl: v.tensor_scalar(
                stats[sl][:, 4:5], stats[sl][:, 2:3], -1.0, None, ALU.mult),
                reads=[r_stats[sl]], writes=[r_stats[sl]], tag="Anm%d" % t)
            P.op("vector", lambda v, sl=sl: v.scalar_tensor_tensor(
                stats[sl][:, 5:6], stats[sl][:, 4:5], stats[sl][:, 2:3],
                stats[sl][:, 3:4], ALU.mult, ALU.add),
                reads=[r_stats[sl]], writes=[r_stats[sl]], tag="Avar%d" % t)
            P.op("scalar", lambda s_, sl=sl: s_.activation(
                stats[sl][:, 6:7], stats[sl][:, 5:6], AF.Ln, bias=fc_sb[:, 0:1]),
                reads=[r_stats[sl], r_consts], writes=[r_stats[sl]], tag="Aln%d" % t)
            P.op("scalar", lambda s_, sl=sl: s_.activation(
                rstd[sl][:, 0:1], stats[sl][:, 6:7], AF.Exp, scale=-0.5),
                reads=[r_stats[sl]], writes=[r_rstd[sl]], tag="Ar%d" % t)
            P.op("vector", lambda v, sl=sl, j=j: v.tensor_scalar(
                xln_bf[sl][:, :], a_stage[j], stats[sl][:, 2:3],
                rstd[sl][:, 0:1], ALU.subtract, ALU.mult),
                reads=a_stage_res[j] + [r_stats[sl], r_rstd[sl]], writes=r_xlnb[sl],
                tag="Aap%d" % t)
            if tl + 4 < 8:
                lda(tl + 4)
            for half in range(2):
                bs = (2 * tl + half) % 4
                pr = r_trb[bs]
                dstv = tr_banks[bs]
                for j in range(8):
                    k = half * 8 + j
                    P.op("tensor", lambda t_, sl=sl, dstv=dstv, j=j, k=k: t_.transpose(
                        dstv[:, j * 128:(j + 1) * 128],
                        xln_bf[sl][:, k * 128:(k + 1) * 128], ident[:, :]),
                        reads=r_xlnb[sl] + [r_consts], writes=[pr],
                        tag="Atr%d_%d" % (t, k))
                dstx = xlnT[:, half * 8:half * 8 + 8, tl * 128:tl * 128 + 128]
                srcx = dstv[:, 0:1024].rearrange("p (a c) -> p a c", a=8)
                if half == 0:
                    P.op("vector", lambda v, dstx=dstx, srcx=srcx: v.tensor_copy(dstx, srcx),
                         reads=[pr], writes=[r_xlnT], tag="Acp%d_0" % t)
                else:
                    P.op("scalar", lambda s_, dstx=dstx, srcx=srcx: s_.copy(dstx, srcx),
                         reads=[pr], writes=[r_xlnT], tag="Acp%d_1" % t)


    a_stage = [xf_buf[0], xf_buf[1], xnf_buf[0], xnf_buf[1]]
    a_stage_res = [r_xf[0], r_xf[1], [r_xd[0], r_xd[1]], [r_xn[0], r_xn[1]]]
    a_stage_hres = [[[r_xf[0][0]], [r_xf[0][1]]],
                    [[r_xf[1][0]], [r_xf[1][1]]],
                    [[r_xd[0]], [r_xd[1]]],
                    [[r_xn[0]], [r_xn[1]]]]

    def phase_a_apply(b, s, H, vkind=None, tile_cb=None):
        xsrc = xres[b]
        if vkind is not None:
            ld_wv_full(vkind)

        def ld(tl):
            t = H * 8 + tl
            j = tl % 4
            P.dma("XA", j, lambda sy, t=t, j=j: sy.dma_start(
                a_stage[j][:, 0:1024], xsrc[t * 128:(t + 1) * 128, 0:1024]),
                reads=[r_hbm[b][t]], writes=a_stage_hres[j][0], tag="ALd%d" % t)
            P.dma("XD", j, lambda sy, t=t, j=j: sy.dma_start(
                a_stage[j][:, 1024:2048], xsrc[t * 128:(t + 1) * 128, 1024:2048]),
                reads=[r_hbm[b][t]], writes=a_stage_hres[j][1], tag="ALd2%d" % t)

        for tl in range(4):
            ld(tl)
        for tl in range(8):
            t = H * 8 + tl
            sl = tl % 2
            j = tl % 4
            cc = t * 16
            nslot = 6 if vkind is not None else (5 if tile_cb is not None else 8)
            for half in range(2):
                P.op("vector", lambda v, sl=sl, cc=cc, j=j, half=half: v.tensor_scalar(
                    xln_bf[sl][:, half * 1024:(half + 1) * 1024],
                    a_stage[j][:, half * 1024:(half + 1) * 1024],
                    stats8[:, cc + 8:cc + 9],
                    stats8[:, cc + 10:cc + 11], ALU.subtract, ALU.mult),
                    reads=a_stage_hres[j][half] + [r_st8[t]],
                    writes=[r_xlnb[sl][half]], tag="AAp%d_%d" % (t, half))
            if tl + 4 < 8:
                ld(tl + 4)
            for half in range(2):
                bs = (2 * tl + half) % nslot
                pr = r_trb[bs]
                dstv = tr_banks[bs]
                for j2 in range(8):
                    k = half * 8 + j2
                    P.op("tensor", lambda t_, sl=sl, dstv=dstv, j2=j2, k=k, half=half: t_.transpose(
                        dstv[:, j2 * 128:(j2 + 1) * 128],
                        xln_bf[sl][:, k * 128:(k + 1) * 128], ident[:, :]),
                        reads=[r_xlnb[sl][half], r_consts], writes=[pr],
                        tag="AAtr%d_%d" % (t, k))
                dstx = xlnT[:, half * 8:half * 8 + 8, tl * 128:tl * 128 + 128]
                srcx = dstv[:, 0:1024].rearrange("p (a c) -> p a c", a=8)
                if half == 0:
                    P.op("vector", lambda v, dstx=dstx, srcx=srcx: v.tensor_copy(dstx, srcx),
                         reads=[pr], writes=[r_xlnT], tag="AAcp%d_0" % t)
                else:
                    P.op("scalar", lambda s_, dstx=dstx, srcx=srcx: s_.copy(dstx, srcx),
                         reads=[pr], writes=[r_xlnT], tag="AAcp%d_1" % t)
            if vkind is not None:
                vb = 6 + (tl % 2)
                vpr = r_trb[vb]
                for k in range(KT):
                    P.op("tensor", lambda t_, k=k, vb=vb, tl=tl: t_.matmul(
                        psB[:, (vb - 4) * 512:(vb - 3) * 512],
                        xlnT[:, k, tl * 128:(tl + 1) * 128], wv_full[:, k, :],
                        start=(k == 0), stop=(k == KT - 1)),
                        reads=[r_xlnT, r_pt], writes=[vpr], tag="AV%d_%d" % (t, k))
                P.op("scalar", lambda s_, vb=vb, t=t: s_.copy(
                    vsb[:, t, :], psB[:, (vb - 4) * 512:(vb - 3) * 512]),
                    reads=[vpr], writes=[r_vsb], tag="AVc%d" % t)
            if tile_cb is not None:
                tile_cb(tl)

    def rope_ops(dst, dst_res, sl, toff, sign_b, tag, split=False):
        src = raw_bf[sl]
        src_res = r_raw[sl]
        pr = r_psA[sl]
        for c in range(2):
            P.op("tensor", lambda t_, c=c, sl=sl: t_.matmul(
                psA[:, sl * 1024 + c * 512: sl * 1024 + (c + 1) * 512],
                swp_c[:, :], src[:, c * 512:(c + 1) * 512],
                start=True, stop=True),
                reads=[src_res, r_consts], writes=[pr], tag=tag + "sw%d" % c)
        P.op("scalar", lambda s_, sl=sl: s_.copy(swp_bf[:, :], psAv[sl]),
             reads=[pr], writes=[r_swpb], tag=tag + "se")
        P.op("vector", lambda v: v.tensor_mul(
            rope_tmp[:, :], src[:, :], tabc[:, toff:toff + 1024]),
            reads=[src_res, r_consts], writes=[r_ropet], tag=tag + "m1")
        P.op("vector", lambda v: v.tensor_mul(
            rope_t2[:, :], swp_bf[:, :], tabs[:, toff:toff + 1024]),
            reads=[r_swpb, r_consts], writes=[r_ropet2], tag=tag + "m2")

        def post():
            P.op("vector", lambda v: v.tensor_tensor(
                dst, rope_tmp[:, :], rope_t2[:, :],
                ALU.subtract if sign_b else ALU.add),
                reads=[r_ropet, r_ropet2], writes=[dst_res], tag=tag + "o")
        if split:
            return post
        post()

    def norm_scale_b_raw(sl, which, idx, tag, split=False):
        bh = [r_psB[2 * sl], r_psB[2 * sl + 1]]
        P.op("scalar", lambda s_, sl=sl: s_.activation(
            sq_sb[:, :], raw_bf[sl][:, :], AF.Square),
            reads=[r_raw[sl]], writes=[r_sq], tag=tag + "sq")
        for c in range(2):
            P.op("tensor", lambda t_, c=c, sl=sl: t_.matmul(
                psB[0:1, sl * 1024 + c * 512: sl * 1024 + (c + 1) * 512],
                ones_col[:, :], sq_sb[:, c * 512:(c + 1) * 512],
                start=True, stop=True),
                reads=[r_sq, r_consts], writes=[bh[c]], tag=tag + "sm%d" % c)
        P.op("vector", lambda v, sl=sl: v.tensor_scalar(
            trow[sl][0:1, :], psB[0:1, sl * 1024:sl * 1024 + 1024],
            1.0 / HD, None, ALU.mult),
            reads=bh, writes=[r_trow[sl]], tag=tag + "msq")
        P.op("scalar", lambda s_, sl=sl: s_.activation(
            trow[sl][0:1, :], trow[sl][0:1, :], AF.Ln, bias=fc_sb[0:1, 1:2]),
            reads=[r_trow[sl], r_consts], writes=[r_trow[sl]], tag=tag + "ln")
        bias_ap = fc_sb[0:1, 2:3] if which == "q" else fc_sb[0:1, 3:4]
        P.op("scalar", lambda s_, sl=sl, bias_ap=bias_ap: s_.activation(
            trow[sl][0:1, :], trow[sl][0:1, :], AF.Exp, scale=-0.5, bias=bias_ap),
            reads=[r_trow[sl], r_consts], writes=[r_trow[sl]], tag=tag + "ex")
        if which == "q":
            P.op("vector", lambda v, sl=sl, idx=idx: v.tensor_scalar(
                trow[sl][0:1, :], trow[sl][0:1, :], gain[0:1, idx:idx + 1],
                None, ALU.mult),
                reads=[r_trow[sl], r_consts], writes=[r_trow[sl]], tag=tag + "g")

        def post():
            for c in range(2):
                P.op("tensor", lambda t_, c=c, sl=sl: t_.matmul(
                    psB[:, sl * 1024 + c * 512: sl * 1024 + (c + 1) * 512],
                    ones_row[:, :], trow[sl][0:1, c * 512:(c + 1) * 512],
                    start=True, stop=True),
                    reads=[r_trow[sl], r_consts], writes=[bh[c]], tag=tag + "bc%d" % c)
        if split:
            return bh, post
        post()
        return bh

    def make_bias(expr_ops, slot, tag, chain=None, defer=None):
        expr_ops()
        nslot = 1 if defer is None else 5 + (slot % 2)
        P.op("vector", lambda v: v.tensor_scalar(
            bnd[0:1, nslot:nslot + 1], bnd[0:1, 0:1], -1.0, None, ALU.mult),
            reads=[r_bnd], writes=[r_bnd], tag=tag + "neg")
        if chain is None:
            bcol, brs = 1536, r_psB[3]
        else:
            bcol, brs = chain * 1024 + 1023, r_psA[chain]

        def post():
            P.op("tensor", lambda t_, bcol=bcol: t_.matmul(
                psA[:, bcol:bcol + 1] if chain is not None else psB[:, bcol:bcol + 1],
                ones_row[:, :], bnd[0:1, nslot:nslot + 1], start=True, stop=True),
                reads=[r_bnd, r_consts], writes=[brs], tag=tag + "bc")
            P.op("vector", lambda v, slot=slot, bcol=bcol: v.tensor_copy(
                bias_sb[:, slot:slot + 1],
                psA[:, bcol:bcol + 1] if chain is not None else psB[:, bcol:bcol + 1]),
                reads=[brs], writes=[r_biasv[slot]], tag=tag + "cp")
        if defer is None:
            post()
        else:
            defer.append(post)

    def phase_b(b, kind, H, do_v=True):
        toff = H * 1024
        is_b = kind == "attn_b"
        if do_v:
            phase_b_v(b, kind, H)
        phase_b_kq_inner(b, kind, H, toff, is_b)

    def phase_b_v(b, kind, H):
        ld_wv_tile(kind, 0, 0)
        ld_wv_tile(kind, 1, 1)
        for k in range(KT):
            wsl = k % 2
            for tl in range(8):
                ps = psA if tl < 4 else psB
                col = (tl % 4) * 512
                pr = r_psA[tl // 2] if tl < 4 else r_psB[tl - 4]
                P.op("tensor", lambda t_, k=k, tl=tl, ps=ps, col=col, wsl=wsl: t_.matmul(
                    ps[:, col:col + 512],
                    xlnT[:, k, tl * 128:(tl + 1) * 128], wv_slab[:, wsl, :],
                    start=(k == 0), stop=(k == KT - 1)),
                    reads=[r_xlnT, r_wv[wsl]], writes=[pr], tag="BV%d_%d" % (k, tl))
            if k + 2 < KT:
                ld_wv_tile(kind, k + 2, wsl)
        for tl in range(8):
            t = H * 8 + tl
            ps = psA if tl < 4 else psB
            col = (tl % 4) * 512
            pr = r_psA[tl // 2] if tl < 4 else r_psB[tl - 4]
            P.op("scalar", lambda s_, ps=ps, col=col, t=t: s_.copy(
                vsb[:, t, :], ps[:, col:col + 512]),
                reads=[pr], writes=[r_vsb], tag="BVc%d" % t)
    def phase_b_kq_inner(b, kind, H, toff, is_b):
        def epilogue_stages(which, idx, sl):
            pr = r_psA[sl]
            dstv = khat[:, idx, toff:toff + 1024] if which == "k" else qhat[:, idx, :]
            dres = r_khat if which == "k" else r_qhat
            st = {}

            def stage1():
                P.op("scalar", lambda s_, sl=sl: s_.copy(raw_bf[sl][:, :], psAv[sl]),
                     reads=[pr], writes=[r_raw[sl]], tag="B%se%d" % (which, idx))
                if is_b:
                    st["bh"], st["np"] = norm_scale_b_raw(
                        sl, "q" if which == "q" else "k", idx,
                        "n%s%d" % (which, idx), split=True)
                    st["rp"] = rope_ops(rope_o[:, :], r_ropeo, sl, toff, True,
                                        "B%sr%d" % (which, idx), split=True)
                else:
                    rope_ops(dstv, dres, sl, toff, False, "B%sr%d" % (which, idx))

            def stage2():
                if is_b:
                    st["np"]()
                    st["rp"]()
                    P.op("vector", lambda v, dstv=dstv, sl=sl: v.tensor_mul(
                        dstv, rope_o[:, :], psB[:, sl * 1024:sl * 1024 + 1024]),
                        reads=[r_ropeo] + st["bh"], writes=[dres],
                        tag="B%ss%d" % (which, idx))
            return stage1, stage2

        heads = [("k", kv) for kv in range(NKV)] + [("q", h) for h in range(NH)]
        ld_wk_head(kind, 0, 0)
        ld_wk_head(kind, 1, 1)
        prev_ep = None
        for hi, (which, idx) in enumerate(heads):
            sl = hi % 2
            pr = r_psA[sl]
            wsl_res = r_wk[sl] if which == "k" else r_wq[sl]
            wsl_view = wk_slab if which == "k" else wq_slab
            for k in range(KT):
                for c in range(2):
                    P.op("tensor", lambda t_, k=k, c=c, sl=sl, wsl_view=wsl_view: t_.matmul(
                        psA[:, sl * 1024 + c * 512: sl * 1024 + (c + 1) * 512],
                        wsl_view[:, sl, k, :], xlnT[:, k, c * 512:(c + 1) * 512],
                        start=(k == 0), stop=(k == KT - 1)),
                        reads=[r_xlnT, wsl_res], writes=[pr],
                        tag="B%s%d_%d_%d" % (which, idx, k, c))
                if k == KT // 2 - 1 and prev_ep is not None:
                    prev_ep[0]()
            if hi + 2 < len(heads):
                w2, i2 = heads[hi + 2]
                if w2 == "k":
                    ld_wk_head(kind, i2, sl)
                else:
                    ld_wq_head(kind, i2, sl)
            if prev_ep is not None:
                prev_ep[1]()
            prev_ep = epilogue_stages(which, idx, sl)
        prev_ep[0]()
        prev_ep[1]()

    def norm_item(which, idx, H):
        toff = H * 1024
        chain = idx % 2
        dstv = khat[:, idx, toff:toff + 1024] if which == "k" else qhat[:, idx, :]
        dres = r_khat if which == "k" else r_qhat
        sqv, sqr = (sq_sb, r_sq) if which == "q" else (rope_tmp, r_ropet)
        co = chain * 1024
        P.op("scalar", lambda s_: s_.activation(sqv[:, :], dstv, AF.Square),
             reads=[dres], writes=[sqr], tag="N%sn%d" % (which, idx))
        for c in range(2):
            P.op("tensor", lambda t_, c=c: t_.matmul(
                psA[0:1, co + c * 512:co + (c + 1) * 512], ones_col[:, :],
                sqv[:, c * 512:(c + 1) * 512], start=True, stop=True),
                reads=[sqr, r_consts], writes=[r_psA[chain]],
                tag="N%snm%d_%d" % (which, idx, c))
        bslot = 2 + chain
        P.op("vector", lambda v: v.tensor_reduce(
            bnd[0:1, bslot:bslot + 1], psA[0:1, co:co + 1024],
            AX.X, ALU.max),
            reads=[r_psA[chain]], writes=[r_bnd],
            tag="N%snr%d" % (which, idx))
        if which == "q":
            P.op("vector", lambda v: v.tensor_copy(
                qn2[0:1, idx:idx + 1], bnd[0:1, bslot:bslot + 1]),
                reads=[r_bnd], writes=[r_qn], tag="Nqnc%d_%d" % (idx, H))
        elif H == 0:
            P.op("vector", lambda v: v.tensor_copy(
                kn2[0:1, idx:idx + 1], bnd[0:1, bslot:bslot + 1]),
                reads=[r_bnd], writes=[r_kn], tag="Nknc%d_%d" % (idx, H))
        else:
            P.op("vector", lambda v: v.tensor_tensor(
                kn2[0:1, idx:idx + 1], kn2[0:1, idx:idx + 1],
                bnd[0:1, bslot:bslot + 1], ALU.max),
                reads=[r_bnd, r_kn], writes=[r_kn], tag="Nknc%d_%d" % (idx, H))

    def attn_bias_a(h):
        def expr():
            kv = h // (NH // NKV)
            P.op("vector", lambda v: v.tensor_tensor(
                bnd[0:1, 2:3], qn2[0:1, h:h + 1], kn2[0:1, kv:kv + 1], ALU.mult),
                reads=[r_qn, r_kn], writes=[r_bnd], tag="bA%dm" % h)
            P.op("scalar", lambda s_: s_.activation(
                bnd[0:1, 3:4], bnd[0:1, 2:3], AF.Ln),
                reads=[r_bnd], writes=[r_bnd], tag="bA%dl" % h)
            P.op("scalar", lambda s_: s_.activation(
                bnd[0:1, 0:1], bnd[0:1, 3:4], AF.Exp, scale=0.5, bias=fc_sb[0:1, 2:3]),
                reads=[r_bnd, r_consts], writes=[r_bnd], tag="bA%de" % h)
        return expr

    def attn_bias_b(h):
        def expr():
            P.op("vector", lambda v: v.tensor_scalar(
                bnd[0:1, 4:5], gain[0:1, h:h + 1], gain[0:1, h:h + 1], 1.0,
                ALU.mult, ALU.add),
                reads=[r_consts], writes=[r_bnd], tag="bB%da" % h)
            P.op("vector", lambda v: v.tensor_scalar(
                bnd[0:1, 0:1], bnd[0:1, 4:5], math.sqrt(HD) / 2.0, None, ALU.mult),
                reads=[r_bnd], writes=[r_bnd], tag="bB%db" % h)
        return expr

    def softmax_tail_steps(h, qc, p2, n, ydst, tag):
        av, sr = r_psB[2 * p2], r_psB[2 * p2 + 1]
        avc = p2 * 1024
        src = p2 * 1024 + 512

        def step1():
            P.op("scalar", lambda s_: s_.activation(
                rc_sb[p2][0:1, 0:n], psB[0:1, src:src + n], AF.Ln),
                reads=[sr], writes=[r_rc[p2]], tag=tag + "rl")
            P.op("scalar", lambda s_: s_.activation(
                rc_sb[p2][0:1, 0:n], rc_sb[p2][0:1, 0:n], AF.Exp, scale=-1.0),
                reads=[r_rc[p2]], writes=[r_rc[p2]], tag=tag + "rc")

        def step2():
            P.op("tensor", lambda t_: t_.matmul(
                psB[:, src:src + n], ones_row[:, :], rc_sb[p2][0:1, 0:n],
                start=True, stop=True),
                reads=[r_rc[p2], r_consts], writes=[sr], tag=tag + "rb")
            P.op("vector", lambda v: v.tensor_copy(
                rbc[p2][:, 0:n], psB[:, src:src + n]),
                reads=[sr], writes=[r_rbc[p2]], tag=tag + "rr")
            P.op("vector", lambda v: v.tensor_mul(
                ydst, psB[:, avc:avc + n], rbc[p2][:, 0:n]),
                reads=[av, r_rbc[p2]], writes=[r_xlnT], tag=tag + "y")
        return step1, step2

    def chunk_gen_a(h, qcl, H, chain):
        kv = h // (NH // NKV)
        bslot = h % 16
        qc = 2 * H + qcl
        av, sr = r_psB[2 * chain], r_psB[2 * chain + 1]
        avc = chain * 1024
        src_ = chain * 1024 + 512
        nkt = 4 * (qc + 1)
        base = chain * 8
        pend_b = []

        def sums_av(kt):
            s = base + (kt % 8)
            g = s // 2
            P.op("tensor", lambda t_, kt=kt, s=s: t_.matmul(
                psB[0:1, src_:src_ + 512], ones_col[:, :], pt[:, s, :],
                start=(kt == 0), stop=(kt == nkt - 1)),
                reads=[r_ptg[g], r_consts], writes=[sr],
                tag="Csm%d_%d_%d" % (h, qc, kt))
            P.op("tensor", lambda t_, kt=kt, s=s: t_.matmul(
                psB[:, avc:avc + 512], vsb[:, kt, kv * 128:(kv + 1) * 128],
                pt[:, s, :], start=(kt == 0), stop=(kt == nkt - 1)),
                reads=[r_ptg[g], r_vsb], writes=[av],
                tag="Cav%d_%d_%d" % (h, qc, kt))

        for kt in range(nkt):
            pcol = chain * 1024 + (kt % 2) * 512
            pg = r_psA[chain]
            s = base + (kt % 8)
            P.op("tensor", lambda t_, kt=kt, pcol=pcol: t_.matmul(
                psA[:, pcol:pcol + 512],
                khat[:, kv, kt * 128:(kt + 1) * 128],
                qhat[:, h, qcl * 512:(qcl + 1) * 512],
                start=True, stop=True),
                reads=[r_khat, r_qhat], writes=[pg],
                tag="CS%d_%d_%d" % (h, qc, kt))
            P.op("scalar", lambda s_, kt=kt, pcol=pcol, s=s: s_.activation(
                pt[:, s, :], psA[:, pcol:pcol + 512],
                AF.Exp, bias=bias_sb[:, bslot:bslot + 1], scale=ISD),
                reads=[pg, r_biasv[bslot]], writes=[r_ptg[s // 2]],
                tag="Ce%d_%d_%d" % (h, qc, kt))
            j = kt - 4 * qc
            if 0 <= j < 4:
                P.op("vector", lambda v, s=s, j=j: v.tensor_mul(
                    pt[:, s, :], pt[:, s, :], mask_a[:, j, :]),
                    reads=[r_ptg[s // 2], r_consts], writes=[r_ptg[s // 2]],
                    tag="Cm%d_%d_%d" % (h, qc, j))
            if kt >= 2:
                sums_av(kt - 2)
            if kt == 2 and qcl == 0 and h + 2 < NH:
                norm_item("q", h + 2, H)
                make_bias(attn_bias_a(h + 2), (h + 2) % 16,
                          "bA%d_%d" % (h + 2, H), chain=chain, defer=pend_b)
            if pend_b and kt == min(nkt - 1, 5):
                pend_b.pop(0)()
            yield
        sums_av(nkt - 2)
        sums_av(nkt - 1)
        ydst = yflat[:, h * 1024 + qcl * 512: h * 1024 + (qcl + 1) * 512]
        s1, s2 = softmax_tail_steps(h, qc, chain, 512, ydst, "C%d_%d" % (h, qc))
        s1()
        yield
        s2()

    def phase_c_a(b, H):
        for kv in range(NKV):
            norm_item("k", kv, H)
        norm_item("q", 0, H)
        make_bias(attn_bias_a(0), 0, "bA0_%d" % H)
        norm_item("q", 1, H)
        make_bias(attn_bias_a(1), 1, "bA1_%d" % H)

        def chain_iter(c):
            for h in range(c, NH, 2):
                for qcl in range(2):
                    yield from chunk_gen_a(h, qcl, H, c)
        its = [chain_iter(0), chain_iter(1)]
        alive = [True, True]
        while any(alive):
            for c in (0, 1):
                if alive[c]:
                    try:
                        next(its[c])
                    except StopIteration:
                        alive[c] = False

    def chunk_gen_b(h, qcl, H, chain):
        kv = h // (NH // NKV)
        bslot = h % 16
        qc = 4 * H + qcl
        c0 = qc * 256
        av, sr = r_psB[2 * chain], r_psB[2 * chain + 1]
        avc = chain * 1024
        src_ = chain * 1024 + 512
        kt0 = max(0, c0 // 128 - 4)
        kts = list(range(kt0, (c0 + 256) // 128))
        npc = len(kts)
        base = chain * 8
        pend_b = []

        def sums_av(i):
            s = base + (i % 8)
            P.op("tensor", lambda t_, i=i, s=s: t_.matmul(
                psB[0:1, src_:src_ + 256], ones_col[:, :],
                ptv[:, s * 512:s * 512 + 256],
                start=(i == 0), stop=(i == npc - 1)),
                reads=[r_ptg[s // 2], r_consts], writes=[sr],
                tag="Esm%d_%d_%d" % (h, qc, i))
            kt = kts[i]
            P.op("tensor", lambda t_, i=i, s=s, kt=kt: t_.matmul(
                psB[:, avc:avc + 256], vsb[:, kt, kv * 128:(kv + 1) * 128],
                ptv[:, s * 512:s * 512 + 256],
                start=(i == 0), stop=(i == npc - 1)),
                reads=[r_ptg[s // 2], r_vsb], writes=[av],
                tag="Eav%d_%d_%d" % (h, qc, i))

        for i, kt in enumerate(kts):
            pcol = chain * 1024 + (i % 2) * 512
            pg = r_psA[chain]
            s = base + (i % 8)
            P.op("tensor", lambda t_, kt=kt, pcol=pcol: t_.matmul(
                psA[:, pcol:pcol + 256],
                khat[:, kv, kt * 128:(kt + 1) * 128],
                qhat[:, h, qcl * 256:(qcl + 1) * 256],
                start=True, stop=True),
                reads=[r_khat, r_qhat], writes=[pg],
                tag="ES%d_%d_%d" % (h, qc, kt))
            P.op("scalar", lambda s_, pcol=pcol, s=s: s_.activation(
                ptv[:, s * 512:s * 512 + 256], psA[:, pcol:pcol + 256],
                AF.Exp, bias=bias_sb[:, bslot:bslot + 1]),
                reads=[pg, r_biasv[bslot]], writes=[r_ptg[s // 2]],
                tag="Ee%d_%d_%d" % (h, qc, i))
            j = kt - (c0 // 128 - 4)
            mi = {0: 0, 1: 1, 4: 2, 5: 3}.get(j)
            if mi is not None:
                P.op("vector", lambda v, s=s, mi=mi: v.tensor_mul(
                    ptv[:, s * 512:s * 512 + 256],
                    ptv[:, s * 512:s * 512 + 256], mask_b[:, mi, :]),
                    reads=[r_ptg[s // 2], r_consts], writes=[r_ptg[s // 2]],
                    tag="Em%d_%d_%d" % (h, qc, i))
            if i >= 2:
                sums_av(i - 2)
            if i == 2 and qcl == 2 and h + 2 < NH:
                make_bias(attn_bias_b(h + 2), (h + 2) % 16,
                          "bB%d_%d" % (h + 2, H), chain=chain, defer=pend_b)
            if pend_b and i == npc - 1:
                pend_b.pop(0)()
            yield
        for i in range(max(0, npc - 2), npc):
            sums_av(i)
        ydst = yflat[:, h * 1024 + qcl * 256: h * 1024 + (qcl + 1) * 256]
        s1, s2 = softmax_tail_steps(h, qc, chain, 256, ydst, "E%d_%d" % (h, qc))
        s1()
        yield
        s2()

    def phase_c_b(b, H):
        make_bias(attn_bias_b(0), 0, "bB0_%d" % H)
        make_bias(attn_bias_b(1), 1, "bB1_%d" % H)

        def chain_iter(c):
            for h in range(c, NH, 2):
                for qcl in range(4):
                    yield from chunk_gen_b(h, qcl, H, c)
        its = [chain_iter(0), chain_iter(1)]
        alive = [True, True]
        while any(alive):
            for c in (0, 1):
                if alive[c]:
                    try:
                        next(its[c])
                    except StopIteration:
                        alive[c] = False

    def phase_d_attn(b, s, kind, H):
        xsrc = xin[b] if s == first_sub else xres[b]
        xdst = xout[b] if s == last_sub else xres[b]
        slab_res = [[r_qhat], [r_qhat], [r_pt],
                    [r_wq[0], r_wq[1], r_wk[0], r_wk[1]]]
        ld_wo_quarter(kind, 1, 2)
        ld_wo_quarter(kind, 2, 0)
        ld_wo_quarter(kind, 3, 1)
        for pair in range(2):
            bufa, bufb = (3, 2) if pair == 0 else (0, 1)
            for tl in range(8):
                t = H * 8 + tl
                sl = tl % 2
                pg = r_psA[sl]
                P.dma("XD", sl, lambda sy, t=t, sl=sl, pair=pair, xsrc=xsrc: sy.dma_start(
                    xd_buf[sl], xsrc[t * 128:(t + 1) * 128, pair * 1024:(pair + 1) * 1024]),
                    reads=[r_hbm[b][t]], writes=[r_xd[sl]], tag="Dxd%d_%d" % (t, pair))
                for k in range(KT):
                    for ii, bf in enumerate((bufa, bufb)):
                        P.op("tensor", lambda t_, k=k, ii=ii, bf=bf, sl=sl, tl=tl: t_.matmul(
                            psA[:, sl * 1024 + ii * 512: sl * 1024 + (ii + 1) * 512],
                            yT[:, k, tl * 128:(tl + 1) * 128], wo_slab[bf][:, k, :],
                            start=(k == 0), stop=(k == KT - 1)),
                            reads=[r_xlnT] + slab_res[bf], writes=[pg],
                            tag="Do%d_%d_%d_%d" % (t, pair, k, ii))
                P.op("vector", lambda v, sl=sl: v.scalar_tensor_tensor(
                    xn_buf[sl], psAv[sl], 1.0, xd_buf[sl], ALU.mult, ALU.add),
                    reads=[pg, r_xd[sl]], writes=[r_xn[sl]], tag="Dev%d_%d" % (t, pair))
                P.dma("ST", sl, lambda sy, t=t, sl=sl, pair=pair, xdst=xdst: sy.dma_start(
                    xdst[t * 128:(t + 1) * 128, pair * 1024:(pair + 1) * 1024],
                    xn_buf[sl]),
                    reads=[r_xn[sl]], writes=[r_hbm[b][t]], tag="Dst%d_%d" % (t, pair))
                if s != last_sub:
                    cc = t * 16
                    P.op("vector", lambda v, sl=sl, cc=cc, pair=pair: v.tensor_scalar(
                        xln_bf[sl][:, 0:1024], xn_buf[sl],
                        1.0, 0.0, ALU.mult, ALU.add,
                        accum_out=stats8[:, cc + pair:cc + pair + 1]),
                        reads=[r_xn[sl]], writes=[r_xlnb[sl][0], r_st8[t]],
                        tag="Dsu%d_%d" % (t, pair))
                    P.op("scalar", lambda s_, sl=sl, cc=cc, pair=pair: s_.activation(
                        sq_sb[:, :], xn_buf[sl], AF.Square,
                        accum_out=stats8[:, cc + 2 + pair:cc + 3 + pair]),
                        reads=[r_xn[sl]], writes=[r_sq, r_st8[t]],
                        tag="Dsq%d_%d" % (t, pair))
                    if pair == 1:
                        stats_chain2(t)

    def router_tile(tl, pend_tr):
        sl = tl % 2
        lgc = 512 + sl * 512
        lgr = r_psB[1 + sl]
        for k in range(KT):
            P.op("tensor", lambda t_, k=k, tl=tl, lgc=lgc: t_.matmul(
                psB[:, lgc:lgc + NE], xlnT[:, k, tl * 128:(tl + 1) * 128],
                rw_sb[:, k, :], start=(k == 0), stop=(k == KT - 1)),
                reads=[r_xlnT, r_lora], writes=[lgr], tag="Rl%d_%d" % (tl, k))
        P.op("vector", lambda v, lgc=lgc: v.tensor_reduce(
            r_m1s[:, 0:1], psB[:, lgc:lgc + NE], AX.X, ALU.max),
            reads=[lgr], writes=[r_rt], tag="Rm1_%d" % tl)
        P.op("vector", lambda v, lgc=lgc: v.tensor_scalar(
            r_k1s[:, :], psB[:, lgc:lgc + NE], r_m1s[:, 0:1], None, ALU.is_ge),
            reads=[lgr, r_rt], writes=[r_rt], tag="Rk1_%d" % tl)
        P.op("vector", lambda v, lgc=lgc: v.scalar_tensor_tensor(
            r_l2s[:, :], r_k1s[:, :], -1e30, psB[:, lgc:lgc + NE],
            ALU.mult, ALU.add),
            reads=[lgr, r_rt], writes=[r_rt], tag="Rl2_%d" % tl)
        P.op("vector", lambda v: v.tensor_reduce(
            r_m2s[:, 0:1], r_l2s[:, :], AX.X, ALU.max),
            reads=[r_rt], writes=[r_rt], tag="Rm2_%d" % tl)
        P.op("vector", lambda v: v.tensor_scalar(
            r_k2s[:, :], r_l2s[:, :], r_m2s[:, 0:1], None, ALU.is_ge),
            reads=[r_rt], writes=[r_rt], tag="Rk2_%d" % tl)
        P.op("scalar", lambda s_: s_.activation(
            r_e2s[:, 0:1], r_m1s[:, 0:1], AF.Exp, bias=r_m2s[:, 0:1], scale=-1.0),
            reads=[r_rt], writes=[r_rt], tag="Re2_%d" % tl)
        P.op("vector", lambda v: v.tensor_scalar(
            r_s2s[:, 0:1], r_e2s[:, 0:1], 1.0, None, ALU.add),
            reads=[r_rt], writes=[r_rt], tag="Rs2_%d" % tl)
        P.op("vector", lambda v: v.reciprocal(r_w1s[:, 0:1], r_s2s[:, 0:1]),
             reads=[r_rt], writes=[r_rt], tag="Rw1_%d" % tl)
        P.op("vector", lambda v: v.tensor_mul(r_w2s[:, 0:1], r_e2s[:, 0:1], r_w1s[:, 0:1]),
             reads=[r_rt], writes=[r_rt], tag="Rw2_%d" % tl)
        P.op("vector", lambda v: v.tensor_scalar(
            r_t8[:, :], r_k1s[:, :], r_w1s[:, 0:1], None, ALU.mult),
            reads=[r_rt], writes=[r_rt], tag="Rt8_%d" % tl)
        P.op("vector", lambda v: v.scalar_tensor_tensor(
            r_wds[:, :], r_k2s[:, :], r_w2s[:, 0:1], r_t8[:, :], ALU.mult, ALU.add),
            reads=[r_rt], writes=[r_rt], tag="Rwd_%d" % tl)
        P.op("vector", lambda v, tl=tl: v.tensor_copy(
            wds8[:, tl * 8:(tl + 1) * 8], r_wds[:, :]),
            reads=[r_rt], writes=[r_wds8[tl]], tag="Rwm_%d" % tl)

        def _tr(tl=tl):
            P.op("tensor", lambda t_, tl=tl: t_.transpose(
                psB[:, 1536:2048].bitcast(BF16)[0:8, tl * 128:(tl + 1) * 128],
                wds8[:, tl * 8:(tl + 1) * 8], ident[:, :]),
                reads=[r_wds8[tl], r_consts], writes=[r_psB[3]], tag="Rtr_%d" % tl)
        pend_tr.append(_tr)
        if len(pend_tr) > 2:
            pend_tr.pop(0)()

    def phase_r_tail(b, H, pend_tr):
        while pend_tr:
            pend_tr.pop(0)()
        P.op("vector", lambda v: v.tensor_copy(
            wdt[0:8, :], psB[:, 1536:2048].bitcast(BF16)[0:8, 0:1024]),
            reads=[r_psB[3]], writes=[r_wdt, r_ropeo], tag="Rwcp")
        for k in range(KT):
            for c in range(2):
                P.op("tensor", lambda t_, k=k, c=c: t_.matmul(
                    psA[:, c * 512:(c + 1) * 512], la_sb[:, k, :],
                    xlnT[:, k, c * 512:(c + 1) * 512],
                    start=(k == 0), stop=(k == KT - 1)),
                    reads=[r_xlnT, r_lora], writes=[r_psA[0]], tag="Rax%d_%d" % (k, c))
        P.op("scalar", lambda s_: s_.copy(axt[:, :], psA[:, 0:1024]),
             reads=[r_psA[0]], writes=[r_axt], tag="Raxc")
        for c in range(2):
            P.op("tensor", lambda t_, c=c: t_.matmul(
                psB[:, c * 512:(c + 1) * 512], rep[:, :], wdt[0:8, c * 512:(c + 1) * 512],
                start=True, stop=True),
                reads=[r_wdt, r_ropeo, r_consts], writes=[r_psB[c]], tag="Rrep%d" % c)
        P.op("vector", lambda v: v.tensor_mul(gt_sb[:, :], axt[:, :], psB[:, 0:1024]),
             reads=[r_axt, r_psB[0], r_psB[1]], writes=[r_gt], tag="Rgt")

    def phase_d_ffn(b, s, H):
        xsrc = xin[b] if s == first_sub else xres[b]
        xdst = xout[b] if s == last_sub else xres[b]
        for tl in range(8):
            t = H * 8 + tl
            sl = tl % 2
            ps = psA if sl == 0 else psB
            prs = [r_psA[0], r_psA[1]] if sl == 0 else [r_psB[0], r_psB[1], r_psB[2], r_psB[3]]
            xor = r_xf[sl]
            xnr = [r_xd[0], r_xd[1]] if sl == 0 else [r_xn[0], r_xn[1]]
            P.dma("XD", sl, lambda sy, t=t, sl=sl, xsrc=xsrc: sy.dma_start(
                xf_buf[sl], xsrc[t * 128:(t + 1) * 128, :]),
                reads=[r_hbm[b][t]], writes=xor, tag="Fxd%d" % t)
            for c in range(4):
                for k in range(KT):
                    P.op("tensor", lambda t_, k=k, c=c, ps=ps, tl=tl: t_.matmul(
                        ps[:, c * 512:(c + 1) * 512],
                        xlnT[:, k, tl * 128:(tl + 1) * 128], wffn[:, k, c * 512:(c + 1) * 512],
                        start=(k == 0), stop=False),
                        reads=[r_xlnT, r_wffn], writes=prs, tag="Fb%d_%d_%d" % (t, c, k))
                P.op("tensor", lambda t_, c=c, ps=ps, tl=tl: t_.matmul(
                    ps[:, c * 512:(c + 1) * 512],
                    gt_sb[:, tl * 128:(tl + 1) * 128], lb_sb[:, c * 512:(c + 1) * 512],
                    start=False, stop=True),
                    reads=[r_gt, r_lora], writes=prs, tag="Fl%d_%d" % (t, c))
            P.op("vector", lambda v, sl=sl, ps=ps: v.scalar_tensor_tensor(
                xnf_buf[sl], ps[:, 0:2048], 1.0, xf_buf[sl], ALU.mult, ALU.add),
                reads=prs + xor, writes=xnr, tag="Fev%d" % t)
            P.dma("ST", sl, lambda sy, t=t, sl=sl, xdst=xdst: sy.dma_start(
                xdst[t * 128:(t + 1) * 128, :], xnf_buf[sl]),
                reads=xnr, writes=[r_hbm[b][t]], tag="Fst%d" % t)
            if s != last_sub:
                cc = t * 16
                P.op("vector", lambda v, sl=sl, cc=cc: v.tensor_scalar(
                    xln_bf[sl][:, :], xnf_buf[sl],
                    1.0, 0.0, ALU.mult, ALU.add,
                    accum_out=stats8[:, cc:cc + 1]),
                    reads=xnr, writes=r_xlnb[sl] + [r_st8[t]], tag="Fsu%d" % t)
                P.op("scalar", lambda s_, sl=sl, cc=cc: s_.activation(
                    psA[:, 0:2048] if sl == 1 else psB[:, 0:2048],
                    xnf_buf[sl], AF.Square,
                    accum_out=stats8[:, cc + 2:cc + 3]),
                    reads=xnr,
                    writes=([r_psA[0], r_psA[1]] if sl == 1 else
                            [r_psB[0], r_psB[1], r_psB[2], r_psB[3]]) + [r_st8[t]],
                    tag="Fsq%d" % t)
                stats_chain1(t)

    for b in range(nb):
        for s in sublayers:
            kind = SUBLAYERS[s]
            if kind == "ffn":
                ld_ffn_weights()
                for H in range(2):
                    pend_tr = []
                    if s != first_sub:
                        phase_a(b, s, H,
                                tile_cb=lambda tl, p=pend_tr: router_tile(tl, p))
                    else:
                        phase_a(b, s, H)
                        for tl in range(8):
                            router_tile(tl, pend_tr)
                    phase_r_tail(b, H, pend_tr)
                    phase_d_ffn(b, s, H)
            else:
                arena_handoff([r_wffn, r_lora, r_gt, r_axt], ALL_BIG_ATTN)
                for H in range(2):
                    vk = kind if s != first_sub else None
                    phase_a(b, s, H, vkind=vk)
                    phase_b(b, kind, H, do_v=(vk is None))
                    ld_wo_quarter(kind, 0, 3)
                    if kind == "attn_a":
                        phase_c_a(b, H)
                    else:
                        phase_c_b(b, H)
                    phase_d_attn(b, s, kind, H)

    P.op("sync", lambda sy: sy.nop(), deps=list(P.last_store.values()), tag="final")
    P.check()

    sem_stack = ExitStack()
    sems = {}
    for name in P.sem_names():
        sems[name] = sem_stack.enter_context(nc.semaphore("s" + name))
    with nc.Block() as block:
        @block.tensor
        def _(t_):
            P.emit_engine(t_, "tensor", sems)

        @block.vector
        def _(v):
            P.emit_engine(v, "vector", sems)

        @block.scalar
        def _(s_):
            P.emit_engine(s_, "scalar", sems)

        @block.gpsimd
        def _(g):
            P.emit_engine(g, "gpsimd", sems)

        @block.sync
        def _(sy):
            P.emit_engine(sy, "sync", sems)

    sem_stack.close()
    st.close()
    nc._dev_aps = {"xout": xout, "xin": xin, "xres": xres}
    nc._dev_prog = P
    return nc


def kernel(**inputs):
    A = host_arrays(inputs)
    x = np.ascontiguousarray(inputs["L0_attn_a_x"], dtype=np.float32)
    nc = build_nc()
    in_maps = []
    for c in range(NCORES):
        m = dict(A)
        m["xin"] = x[c * NB:(c + 1) * NB]
        in_maps.append(m)
    res = run_bass_kernel_spmd(nc, in_maps, core_ids=list(range(NCORES)))
    out = np.concatenate([r["xout"] for r in res.results], axis=0)
    return out.astype(np.float32)
